# version 03 (final)
# speedup vs baseline: 1.0476x; 1.0399x over previous
import contextlib
import numpy as np
import ml_dtypes
import concourse.bass as bass
import concourse.mybir as mybir
from concourse.alu_op_type import AluOpType
from concourse.bass_utils import run_bass_kernel_spmd

F32 = mybir.dt.float32
I32 = mybir.dt.int32
BF16 = mybir.dt.bfloat16
AF = mybir.ActivationFunctionType
AX = mybir.AxisListType
OP = AluOpType
BFD = ml_dtypes.bfloat16

B, S, D = 4, 4096, 1024
T = 2048
NT = T // 128
ND = D // 128
PAIRS = [[0, 1], [2, 3], [4, 5], [6, 7]]
LN_EPS = 1e-5
BIG = 1.0e9

ENGS = ("sync", "vector", "scalar", "tensor", "gpsimd")
DMA_Q = {"sync": "dq_sync", "scalar": "dq_scalar", "gpsimd": "dq_pool"}


class Buf:
    __slots__ = ("name", "last_writer", "readers")

    def __init__(self, name):
        self.name = name
        self.last_writer = None
        self.readers = []


class _Op:
    __slots__ = ("idx", "engine", "fn", "deps", "kind", "sem_key", "milestone",
                 "need_inc", "tag")

    def __init__(self, idx, engine, fn, kind, tag):
        self.idx = idx
        self.engine = engine
        self.fn = fn
        self.deps = []
        self.kind = kind
        self.sem_key = None
        self.milestone = None
        self.need_inc = False
        self.tag = tag


class Sched:
    def __init__(self, nc, es):
        self.nc = nc
        self.es = es
        self.es_stack = [es]
        self.ops = []
        self._uid = 0
        self.scope_bufs = [[]]

    def u(self, base):
        self._uid += 1
        return f"{base}{self._uid}"

    def buf(self, name):
        b = Buf(name)
        self.scope_bufs[-1].append(b)
        return b

    def sbuf(self, name, shape, dt=F32, up=0, side=None):
        es = self.es if up == 0 else self.es_stack[-1 - up]
        kw = {"side": side} if side else {}
        t = es.enter_context(self.nc.sbuf_tensor(self.u(name), list(shape), dt,
                                                 **kw))
        b = Buf(name)
        self.scope_bufs[-1 - up].append(b)
        return t, b

    def psum(self, name, shape, dt=F32):
        t = self.es.enter_context(self.nc.psum_tensor(self.u(name), list(shape), dt))
        return t, self.buf(name)

    @contextlib.contextmanager
    def scope(self):
        es = contextlib.ExitStack()
        old = self.es
        self.es = es
        self.es_stack.append(es)
        self.scope_bufs.append([])
        try:
            with es:
                yield es
                self.fence(self.scope_bufs[-1])
        finally:
            self.scope_bufs.pop()
            self.es_stack.pop()
            self.es = old

    def op(self, engine, fn, reads=(), writes=(), after=(), kind="compute", tag=""):
        o = _Op(len(self.ops), engine, fn, kind, tag)
        for b in reads:
            if b.last_writer is not None:
                o.deps.append(b.last_writer)
            b.readers.append(o)
        for b in writes:
            if b.last_writer is not None:
                o.deps.append(b.last_writer)
            for r in b.readers:
                if r is not o:
                    o.deps.append(r)
            b.last_writer = o
            b.readers = []
        for a in after:
            if a is not None:
                o.deps.append(a)
        self.ops.append(o)
        return o

    def dma(self, q, out, in_, reads=(), writes=(), after=(), tag="", **kw):
        return self.op(q, (lambda e, _o=out, _i=in_, _k=dict(kw):
                           e.dma_start(out=_o, in_=_i, **_k)),
                       reads=reads, writes=writes, after=after, kind="dma", tag=tag)

    def cc(self, kind, op_, groups, ins, outs, reads=(), writes=(), tag="cc"):
        def fn(e):
            return e.collective_compute(kind, op_, replica_groups=groups,
                                        ins=ins, outs=outs)
        return self.op("gpsimd", fn, reads=reads, writes=writes, kind="cc", tag=tag)

    def fence(self, bufs, tag="fence"):
        deps = []
        for b in bufs:
            if b.last_writer is not None:
                deps.append(b.last_writer)
            deps.extend(b.readers)
        if not deps:
            return
        ph1 = [self.op(e, lambda eng: eng.nop(), after=deps, tag=tag)
               for e in ENGS]
        for e in ENGS:
            self.op(e, lambda eng: eng.nop(), after=ph1, tag=tag + "2")

    def finalize(self):
        nc = self.nc
        for o in self.ops:
            nd = []
            for d in o.deps:
                if (d.kind == "compute" and o.kind == "compute"
                        and d.engine == o.engine and o.engine in ("tensor", "sync")):
                    continue
                d.need_inc = True
                nd.append(d)
            o.deps = nd
        counters = {}
        n_cc = 0
        for o in self.ops:
            if o.kind == "cc":
                o.sem_key = f"cc_sem{n_cc}"
                n_cc += 1
                o.milestone = 1
                counters[o.sem_key] = 1
                o.need_inc = True
                continue
            if o.kind == "dma":
                q = DMA_Q[o.engine]
                counters[q] = counters.get(q, 0) + 16
                if o.need_inc:
                    dk = "dk_" + o.engine
                    counters[dk] = counters.get(dk, 0) + 16
                    o.sem_key = dk
                    o.milestone = counters[dk]
                o.tag = (o.tag, counters[q])
            else:
                o.sem_key = "en_" + o.engine
                if o.need_inc:
                    counters[o.sem_key] = counters.get(o.sem_key, 0) + 16
                    o.milestone = counters[o.sem_key]
        sems = {k: self.es.enter_context(nc.semaphore(k)) for k in counters}
        per_eng = {e: [o for o in self.ops if o.engine == e] for e in ENGS}

        with nc.Block() as blk:
            def emit_for(engine):
                def body(eng):
                    wm = {}
                    for o in per_eng[engine]:
                        need = {}
                        for d in o.deps:
                            k = d.sem_key
                            if need.get(k, 0) < d.milestone:
                                need[k] = d.milestone
                        for k, v in need.items():
                            if wm.get(k, 0) < v:
                                eng.wait_ge(sems[k], v)
                                wm[k] = v
                        inst = o.fn(eng)
                        if o.kind == "dma":
                            q = DMA_Q[o.engine]
                            qcum = o.tag[1]
                            inst.then_inc(sems[q], 16)
                            if o.need_inc:
                                eng.wait_ge(sems[q], qcum)
                                wm[q] = max(wm.get(q, 0), qcum)
                                eng.sem_inc(sems[o.sem_key], 16)
                        elif o.need_inc:
                            if o.kind == "cc":
                                inst.then_inc(sems[o.sem_key])
                            else:
                                inst.then_inc(sems[o.sem_key], 16)
                return body
            for e in ENGS:
                getattr(blk, e)(emit_for(e))



def host_consts():
    c = {}
    c["ident_bf"] = np.eye(128, dtype=BFD)
    c["ident_f32"] = np.eye(128, dtype=np.float32)
    i = np.arange(128, dtype=np.float64)
    DEC = 0.99
    jj, ii = np.meshgrid(i, i, indexing="ij")
    c["dec_tri_T"] = np.where(jj < ii, 0.01 * DEC ** (ii - 1 - jj), 0.0).astype(BFD)
    c["dec_pow"] = (DEC ** i).astype(np.float32)[:, None]
    c["dec_tail"] = (0.01 * DEC ** (127 - i)).astype(np.float32)[:, None]
    selq = np.zeros((128, 8, 2, 128), dtype=BFD)
    for m in range(8):
        for dst in range(128):
            hh = dst % 32
            if hh < 16:
                selq[hh * 8 + m, m, 0, dst] = 1.0
            else:
                selq[(hh - 16) * 8 + m, m, 1, dst] = 1.0
    c["sel_q"] = selq
    eloc = np.zeros((16, 4, 128), dtype=np.float32)
    for ci in range(4):
        for p in range(128):
            eloc[ci * 4 + p // 32, ci, p] = 1.0
    c["e_loc"] = eloc
    erep = np.zeros((32, 128), dtype=np.float32)
    for p in range(128):
        erep[p % 32, p] = 1.0
    c["e_rep"] = erep
    return c


L0_PARAM_SPECS = [
    ("l0w_k", [D, 512]), ("l0w_s", [D, 16]),
    ("l0w_qre", [D, 512]), ("l0w_qim", [D, 512]),
    ("l0ck_k", [512, 4]), ("l0ck_s", [16, 4]),
    ("l0theta", [512, 8]), ("l0slope", [16]), ("l0kksel", [16]),
    ("l0sscale", [16]), ("l0wout", [512, D]),
]


def l0_core_params(inputs, h):
    w_in = np.ascontiguousarray(inputs["L0_attn_W_in"], dtype=np.float32)
    conv = np.ascontiguousarray(inputs["L0_attn_conv_k"],
                                dtype=np.float32).reshape(4, 1056)
    theta = np.ascontiguousarray(inputs["L0_attn_theta"],
                                 dtype=np.float32).reshape(1024, 8)
    dec = np.ascontiguousarray(inputs["L0_attn_decay_slopes"],
                               dtype=np.float32)
    anc = np.ascontiguousarray(inputs["L0_attn_anchor_slopes"],
                               dtype=np.float32)
    ssc = np.ascontiguousarray(inputs["L0_attn_score_scale"],
                               dtype=np.float32)
    wout = np.ascontiguousarray(inputs["L0_attn_W_out"], dtype=np.float32)
    qd = np.arange(512)
    cols_re = 1056 + 2 * (512 * h + qd)
    m = {
        "l0w_k": np.ascontiguousarray(w_in[:, 512 * h:512 * h + 512]),
        "l0w_s": np.ascontiguousarray(w_in[:, 1024 + 16 * h:1024 + 16 * h + 16]),
        "l0w_qre": np.ascontiguousarray(w_in[:, cols_re]),
        "l0w_qim": np.ascontiguousarray(w_in[:, cols_re + 1]),
        "l0ck_k": np.ascontiguousarray(conv[:, 512 * h:512 * h + 512].T),
        "l0ck_s": np.ascontiguousarray(
            conv[:, 1024 + 16 * h:1024 + 16 * h + 16].T),
        "l0theta": np.ascontiguousarray(theta[512 * h:512 * h + 512, :]),
        "l0slope": (np.ascontiguousarray(dec[0:16]) if h == 0 else
                    np.ascontiguousarray(np.concatenate([dec[16:28], anc]))),
        "l0kksel": (np.zeros(16, dtype=np.float32) if h == 0 else
                    np.array([0.0] * 12 + [1.0] * 4, dtype=np.float32)),
        "l0sscale": np.ascontiguousarray(ssc[16 * h:16 * h + 16]),
        "l0wout": np.ascontiguousarray(wout[512 * h:512 * h + 512, :]),
    }
    return m


INPUT_SPECS = [
    ("L0_attn_W_in", [D, 3104]), ("L0_attn_conv_k", [4, 1, 1056]),
    ("L0_attn_theta", [1, 1, 32, 32, 8]), ("L0_attn_decay_slopes", [28]),
    ("L0_attn_anchor_slopes", [4]), ("L0_attn_score_scale", [32]),
    ("L0_attn_W_re", [32, 32]), ("L0_attn_W_im", [32, 32]),
    ("L0_attn_norm_scale", [64]), ("L0_attn_W_out", [D, D]),
    ("L0_ffn_router_w", [8, D]), ("L0_ffn_router_b", [8]),
    ("L0_ffn_expert_w", [8, D, D]), ("L0_ffn_expert_b", [8, D]),
    ("L1_attn_ln_w", [D]), ("L1_attn_ln_b", [D]),
    ("L1_attn_tmk", [D]), ("L1_attn_tmv", [D]), ("L1_attn_tmr", [D]),
    ("L1_attn_decay_w", [D]),
    ("L1_attn_Wk", [D, D]), ("L1_attn_Wv", [D, D]),
    ("L1_attn_Wr", [D, D]), ("L1_attn_Wo", [D, D]),
    ("L1_ffn_gate_w", [8, D]), ("L1_ffn_gate_bias", [8]),
    ("L1_ffn_w1", [8, D, D]), ("L1_ffn_w3", [8, D, D]), ("L1_ffn_w2", [8, D, D]),
    ("L1_ffn_sw1", [2048, D]), ("L1_ffn_sw3", [2048, D]), ("L1_ffn_sw2", [D, 2048]),
    ("L2_attn_Wk", [128, D]), ("L2_attn_bk", [128]),
    ("L2_attn_Wv", [128, D]), ("L2_attn_bv", [128]),
    ("L2_attn_Wq", [128, D]), ("L2_attn_bq", [128]),
    ("L2_attn_Wo", [D, 128]), ("L2_attn_bo", [D]),
    ("L2_ffn_router_w", [D, 16]),
    ("L2_ffn_w1", [16, D, 2048]), ("L2_ffn_b1", [16, 2048]),
    ("L2_ffn_w2", [16, 2048, D]), ("L2_ffn_b2", [16, D]),
    ("L3_attn_W_qkv", [3072, D]), ("L3_attn_b_qkv", [3072]),
    ("L3_attn_W_o", [D, D]), ("L3_attn_b_o", [D]),
    ("L3_ffn_w_gate", [2816, D]), ("L3_ffn_w_up", [2816, D]),
    ("L3_ffn_w_down", [D, 2816]),
]


class Env:
    pass



def w_transpose(env, w_dram, n_out, n_in, tag, dst=None):
    sch = env.sch
    no_t = n_out // 128
    ni_c = n_in // 128
    if dst is None:
        wt_t, b_wt = sch.sbuf("wt" + tag, [128, ni_c, n_out], BF16)
    else:
        wt_t, b_wt = dst
    with sch.scope():
        wrow_f, _ = sch.sbuf("wrf", [128, 2, n_in], F32)
        wrow, _ = sch.sbuf("wrb", [128, 2, n_in], BF16)
        ps, _ = sch.psum("pstw", [128, 2, 1024], BF16)
        bufs_f = [sch.buf("wrf0"), sch.buf("wrf1")]
        bufs_b = [sch.buf("wrb0"), sch.buf("wrb1")]
        bufs_p = [sch.buf("pstw0"), sch.buf("pstw1")]
        ngrp = (ni_c + 7) // 8
        g = 0
        for i in range(no_t):
            sl = i % 2
            sch.dma("sync", wrow_f[:, sl, :], w_dram[i * 128:(i + 1) * 128, :],
                    writes=[bufs_f[sl]], tag="wld")
            sch.op("vector",
                   (lambda e, s=sl: e.tensor_copy(wrow[:, s, :], wrow_f[:, s, :])),
                   reads=[bufs_f[sl]], writes=[bufs_b[sl]], tag="wcast")
            for gi in range(ngrp):
                cg = gi * 8
                n = min(8, ni_c - cg)
                psl = g % 2
                g += 1
                for c in range(cg, cg + n):
                    sch.op("tensor",
                           (lambda e, s=sl, pq=psl, cc=c, c0=cg: e.transpose(
                               ps[:, pq, (cc - c0) * 128:(cc - c0 + 1) * 128],
                               wrow[:, s, cc * 128:(cc + 1) * 128], env.ident_bf[:, :])),
                           reads=[bufs_b[sl], env.b_ident_bf],
                           writes=[bufs_p[psl]], tag="wtr")
                sch.op("scalar",
                       (lambda e, pq=psl, c0=cg, n_=n, ii=i: e.copy(
                           wt_t[:, c0:c0 + n_, ii * 128:(ii + 1) * 128],
                           ps[:, pq, 0:n_ * 128].rearrange("p (c q) -> p c q", q=128))),
                       reads=[bufs_p[psl]], writes=[b_wt], tag="wev")
    return wt_t, b_wt


def bcast_load(env, src_ap, width, tag, dt=F32):
    sch = env.sch
    t, b = sch.sbuf("bc" + tag, [128, width], dt)
    sch.dma("sync", t[:, :], src_ap.partition_broadcast(128), writes=[b], tag="bc")
    return t, b


def fm_load(env, src_ap, width, tag, dt=F32):
    sch = env.sch
    n = width // 128
    t, b = sch.sbuf("fv" + tag, [128, n], dt)
    sch.dma("sync", t[:, :], src_ap.rearrange("(c p) -> p c", p=128),
            writes=[b], tag="fv", allow_slow_non_contiguous=True)
    return t, b



def emit_ln_phase(env, xi, xi_buf, want, inner_wb=None, router=None,
                  prealloc=None, rms_out=None):
    sch = env.sch
    nc = env.nc
    out = {}
    prealloc = prealloc or {}
    for name, mode in want.items():
        if name == "logits":
            continue
        if name in prealloc:
            out[name] = prealloc[name]
            continue
        shape = [128, ND, T] if name.endswith("_fm") else [128, NT, D]
        if mode == "sbuf":
            out[name] = sch.sbuf(name, shape, BF16)
        else:
            ap = nc.dram_tensor(sch.u("d_" + name), [128, shape[1] * shape[2]],
                                BF16).ap()
            out[name] = (ap, sch.buf("d_" + name))
    if "logits" in want:
        if "logits" in prealloc:
            out["logits"] = prealloc["logits"]
        else:
            E = router[2]
            out["logits"] = sch.sbuf("logits", [128, NT, E], F32)

    with sch.scope():
        xt, _ = sch.sbuf("xt", [128, 2, D], F32)
        xnt, _ = sch.sbuf("xnt", [128, 2, D], F32)
        xnt_b, _ = sch.sbuf("xntb", [128, 2, D], BF16)
        sq, b_sq = sch.sbuf("sq", [128, D], F32)
        st, _ = sch.sbuf("stt", [128, 2, 16], F32)
        b_xt = [sch.buf("xt0"), sch.buf("xt1")]
        b_xn = [sch.buf("xn0"), sch.buf("xn1")]
        b_xnb = [sch.buf("xnb0"), sch.buf("xnb1")]
        b_st = [sch.buf("st0"), sch.buf("st1")]
        fmstage, _ = sch.sbuf("fmst", [128, 2, ND, 128], BF16)
        b_fmst = [sch.buf("fmst0"), sch.buf("fmst1")]
        ps_tr, _ = sch.psum("pstr", [128, 2, 1024], BF16)
        b_ps = [sch.buf("pstr0"), sch.buf("pstr1")]
        if "xnl_fm" in want:
            xnlt, _ = sch.sbuf("xnlt", [128, 2, D], F32)
            xnlt_b, _ = sch.sbuf("xnlb", [128, 2, D], BF16)
            b_xnl = [sch.buf("xnl0"), sch.buf("xnl1")]
            b_xnlb = [sch.buf("xnlb0"), sch.buf("xnlb1")]
            fmstage2, _ = sch.sbuf("fmst2", [128, 2, ND, 128], BF16)
            b_fmst2 = [sch.buf("fm2a"), sch.buf("fm2b")]
            ps_tr2, _ = sch.psum("pstr2", [128, 2, 1024], BF16)
            b_ps2 = [sch.buf("ps2a"), sch.buf("ps2b")]
        if rms_out is not None:
            rms_t, b_rms, rms_eps = rms_out
            xnr, _ = sch.sbuf("xnr", [128, 2, D], BF16)
            b_xnr = [sch.buf("xnr0"), sch.buf("xnr1")]
            ps_tr9, _ = sch.psum("pstr9", [128, 2, 1024], BF16)
            b_ps9 = [sch.buf("ps9a"), sch.buf("ps9b")]
            fmstage9, _ = sch.sbuf("fmst9", [128, 2, ND, 128], BF16)
            b_fmst9 = [sch.buf("fm9a"), sch.buf("fm9b")]
        if router is not None:
            rw_t, rw_b, E = router
            fm32, _ = sch.sbuf("fm32", [128, 2, D], F32)
            b_fm32 = [sch.buf("fm32a"), sch.buf("fm32b")]
            ps_trf, _ = sch.psum("pstrf", [128, 2, 512], F32)
            b_psf = [sch.buf("psfa"), sch.buf("psfb")]
            ps_lg, _ = sch.psum("pslg", [128, max(E, 8)], F32)
            b_pslg = sch.buf("pslg")

        def ln_pass(src_t, src_buf, dst_t, dst_buf, bst, stv, s0):
            sch.op("scalar", (lambda e, s=src_t, d=stv: e.activation(
                sq[:, :], s, AF.Square, accum_out=d[:, s0:s0 + 1])),
                   reads=[src_buf], writes=[b_sq, bst], tag="lnsq")
            sch.op("vector", (lambda e, s=src_t, d=stv: e.tensor_reduce(
                d[:, s0 + 1:s0 + 2], s, AX.X, OP.add)),
                   reads=[src_buf], writes=[bst], tag="lnsum")
            sch.op("vector", (lambda e, d=stv: e.tensor_scalar(
                d[:, s0 + 2:s0 + 3], d[:, s0 + 1:s0 + 2], 1.0 / D, None, OP.mult)),
                   reads=[bst], writes=[bst])
            sch.op("vector", (lambda e, d=stv: e.tensor_tensor(
                d[:, s0 + 3:s0 + 4], d[:, s0 + 2:s0 + 3], d[:, s0 + 2:s0 + 3],
                OP.mult)), reads=[bst], writes=[bst])
            sch.op("vector", (lambda e, d=stv: e.tensor_scalar(
                d[:, s0 + 4:s0 + 5], d[:, s0:s0 + 1], 1.0 / D, None, OP.mult)),
                   reads=[bst], writes=[bst])
            sch.op("vector", (lambda e, d=stv: e.scalar_tensor_tensor(
                d[:, s0 + 5:s0 + 6], d[:, s0 + 4:s0 + 5], LN_EPS,
                d[:, s0 + 3:s0 + 4], OP.add, OP.subtract)),
                   reads=[bst], writes=[bst], tag="vareps")
            sch.op("scalar", (lambda e, d=stv: e.activation(
                d[:, s0 + 6:s0 + 7], d[:, s0 + 5:s0 + 6], AF.Sqrt)),
                   reads=[bst], writes=[bst], tag="sqrt")
            sch.op("vector", (lambda e, d=stv: e.reciprocal(
                d[:, s0 + 7:s0 + 8], d[:, s0 + 6:s0 + 7])),
                   reads=[bst], writes=[bst], tag="recip")
            sch.op("vector", (lambda e, s=src_t, o=dst_t, d=stv:
                              e.scalar_tensor_tensor(
                                  o, s, d[:, s0 + 2:s0 + 3],
                                  d[:, s0 + 7:s0 + 8].broadcast_to([128, D]),
                                  OP.subtract, OP.mult)),
                   reads=[src_buf, bst], writes=[dst_buf], tag="lnout")

        def do_transposes(src_t, src_buf, pst, b_pst, stage_t, b_stage, sl):
            for c in range(ND):
                sch.op("tensor", (lambda e, s=src_t, cc=c, q=pst, qs=sl:
                                  e.transpose(q[:, qs, cc * 128:(cc + 1) * 128],
                                              s[:, cc * 128:(cc + 1) * 128],
                                              env.ident_bf[:, :])),
                       reads=[src_buf, env.b_ident_bf], writes=[b_pst], tag="xtr")
            sch.op("scalar", (lambda e, q=pst, qs=sl, g=stage_t, gs=sl: e.copy(
                g[:, gs, :, :],
                q[:, qs, :].rearrange("p (c q2) -> p c q2", q2=128))),
                   reads=[b_pst], writes=[b_stage], tag="xtev")

        def sink_fm(name, stage_t, b_stage, sl, j):
            hnd, b_h = out[name]
            if want[name] == "sbuf":
                sch.op("vector", (lambda e, g=stage_t, gs=sl, t_=hnd, jj=j:
                                  e.tensor_copy(t_[:, :, jj * 128:(jj + 1) * 128],
                                                g[:, gs, :, :])),
                       reads=[b_stage], writes=[b_h], tag="fmres")
            else:
                view = hnd.rearrange("p (c t2) -> p c t2", t2=T)
                sch.dma("sync", view[:, :, j * 128:(j + 1) * 128],
                        stage_t[:, sl, :, :], reads=[b_stage], writes=[b_h],
                        tag="fmdram")

        for j in range(NT):
            sl = j % 2
            sch.dma("sync", xt[:, sl, :], xi[j * 128:(j + 1) * 128, :],
                    reads=[xi_buf], writes=[b_xt[sl]], tag="xld")
            ln_pass(xt[:, sl, :], b_xt[sl], xnt[:, sl, :], b_xn[sl], b_st[sl],
                    st[:, sl, :], 0)
            sch.op("vector", (lambda e, s=sl: e.tensor_copy(xnt_b[:, s, :],
                                                            xnt[:, s, :])),
                   reads=[b_xn[sl]], writes=[b_xnb[sl]], tag="xncast")
            if "xn_tm" in want:
                hnd, b_h = out["xn_tm"]
                if want["xn_tm"] == "sbuf":
                    sch.op("scalar", (lambda e, s=sl, jj=j, t_=hnd: e.copy(
                        t_[:, jj, :], xnt_b[:, s, :])),
                           reads=[b_xnb[sl]], writes=[b_h], tag="tmres")
                else:
                    view = hnd.rearrange("p (j2 d2) -> p j2 d2", d2=D)
                    sch.dma("sync", view[:, j, :], xnt_b[:, sl, :],
                            reads=[b_xnb[sl]], writes=[b_h], tag="tmdram")
            if "xn_fm" in want:
                do_transposes(xnt_b[:, sl, :], b_xnb[sl], ps_tr, b_ps[sl],
                              fmstage, b_fmst[sl], sl)
                sink_fm("xn_fm", fmstage, b_fmst[sl], sl, j)
            if router is not None:
                t_lg, b_lg = out["logits"]
                for half in range(2):
                    for c in range(4):
                        cc = half * 4 + c
                        sch.op("tensor", (lambda e, s=sl, c_=c, cc_=cc:
                                          e.transpose(
                                              ps_trf[:, s, c_ * 128:(c_ + 1) * 128],
                                              xnt[:, s, cc_ * 128:(cc_ + 1) * 128],
                                              env.ident_f32[:, :])),
                               reads=[b_xn[sl], env.b_ident_f32],
                               writes=[b_psf[sl]], tag="ftr")
                    sch.op("scalar", (lambda e, s=sl, h=half: e.copy(
                        fm32[:, s, h * 512:(h + 1) * 512], ps_trf[:, s, :])),
                           reads=[b_psf[sl]], writes=[b_fm32[sl]], tag="fev")
                for c in range(ND):
                    sch.op("tensor", (lambda e, s=sl, cc=c, EE=E: e.matmul(
                        ps_lg[:, 0:EE], fm32[:, s, cc * 128:(cc + 1) * 128],
                        rw_t[:, cc, :], start=(cc == 0), stop=(cc == ND - 1))),
                           reads=[b_fm32[sl], rw_b], writes=[b_pslg], tag="lgmm")
                sch.op("vector", (lambda e, jj=j, EE=E: e.tensor_copy(
                    t_lg[:, jj, :], ps_lg[:, 0:EE])),
                       reads=[b_pslg], writes=[out["logits"][1]], tag="lgev")
            if rms_out is not None:
                if RMS_TRUNC >= 1:
                  sch.op("scalar", (lambda e, s=sl, d=st: e.activation(
                    sq[:, :], xnt[:, s, :], AF.Square,
                    accum_out=d[:, s, 8:9])),
                       reads=[b_xn[sl]], writes=[b_sq, b_st[sl]], tag="rmssq")
                if RMS_TRUNC >= 2:
                  sch.op("vector", (lambda e, s=sl, d=st: e.tensor_scalar(
                    d[:, s, 9:10], d[:, s, 8:9], 1.0 / D, rms_eps,
                    OP.mult, OP.add)),
                       reads=[b_st[sl]], writes=[b_st[sl]], tag="rmsms")
                if RMS_TRUNC >= 3:
                  sch.op("scalar", (lambda e, s=sl, d=st: e.activation(
                    d[:, s, 10:11], d[:, s, 9:10], AF.Sqrt)),
                       reads=[b_st[sl]], writes=[b_st[sl]], tag="rmssqrt")
                if RMS_TRUNC >= 4:
                  sch.op("vector", (lambda e, s=sl, d=st: e.reciprocal(
                    d[:, s, 11:12], d[:, s, 10:11])),
                       reads=[b_st[sl]], writes=[b_st[sl]], tag="rmsrec")
                if RMS_TRUNC >= 5:
                  sch.op("vector", (lambda e, s=sl, d=st: e.tensor_scalar(
                    xnr[:, s, :], xnt[:, s, :], d[:, s, 11:12], None,
                    OP.mult)),
                       reads=[b_xn[sl], b_st[sl]], writes=[b_xnr[sl]],
                       tag="rmsout")
                if RMS_TRUNC >= 6:
                  for c in range(ND):
                    sch.op("tensor", (lambda e, s=sl, cc=c: e.transpose(
                        ps_tr9[:, s, cc * 128:(cc + 1) * 128],
                        xnr[:, s, cc * 128:(cc + 1) * 128],
                        env.ident_bf[:, :])),
                           reads=[b_xnr[sl], env.b_ident_bf],
                           writes=[b_ps9[sl]], tag="rtr")
                if RMS_TRUNC >= 7:
                  sch.op("vector", (lambda e, s=sl: e.tensor_copy(
                    fmstage9[:, s, :, :],
                    ps_tr9[:, s, :].rearrange("p (c q2) -> p c q2", q2=128))),
                       reads=[b_ps9[sl]], writes=[b_fmst9[sl]], tag="rtev")
                if RMS_TRUNC >= 8:
                  sch.op("vector", (lambda e, s=sl, jj=j: e.tensor_copy(
                    rms_t[:, :, jj * 128:(jj + 1) * 128],
                    fmstage9[:, s, :, :])),
                       reads=[b_fmst9[sl]], writes=[b_rms], tag="rmsres")
            if "xnl_fm" in want:
                w_bc, b_wbc, bb_bc, b_bbc = inner_wb
                ln_pass(xnt[:, sl, :], b_xn[sl], xnlt[:, sl, :], b_xnl[sl],
                        b_st[sl], st[:, sl, :], 8)
                sch.op("vector", (lambda e, s=sl: e.tensor_tensor(
                    xnlt[:, s, :], xnlt[:, s, :], w_bc[:, :], OP.mult)),
                       reads=[b_xnl[sl], b_wbc], writes=[b_xnl[sl]], tag="lnw")
                sch.op("vector", (lambda e, s=sl: e.tensor_tensor(
                    xnlt[:, s, :], xnlt[:, s, :], bb_bc[:, :], OP.add)),
                       reads=[b_xnl[sl], b_bbc], writes=[b_xnl[sl]], tag="lnb")
                sch.op("vector", (lambda e, s=sl: e.tensor_copy(
                    xnlt_b[:, s, :], xnlt[:, s, :])),
                       reads=[b_xnl[sl]], writes=[b_xnlb[sl]], tag="xnlcast")
                do_transposes(xnlt_b[:, sl, :], b_xnlb[sl], ps_tr2, b_ps2[sl],
                              fmstage2, b_fmst2[sl], sl)
                sink_fm("xnl_fm", fmstage2, b_fmst2[sl], sl, j)
    return out


def emit_combine(env, xi, xi_buf, xo, xo_buf, s_d, b_sd, yacc, b_yacc):
    sch = env.sch
    with sch.scope():
        xld, _ = sch.sbuf("xld", [128, 2, D], F32)
        sld, _ = sch.sbuf("sld", [128, 2, D], BF16)
        gt, _ = sch.sbuf("gt", [128, 2, D], BF16)
        mt, _ = sch.sbuf("mt", [128, 2, D], F32)
        ot, _ = sch.sbuf("ot", [128, 2, D], F32)
        b_x = [sch.buf("cx0"), sch.buf("cx1")]
        b_s = [sch.buf("cs0"), sch.buf("cs1")]
        b_g = [sch.buf("cg0"), sch.buf("cg1")]
        b_m = [sch.buf("cm0"), sch.buf("cm1")]
        b_o = [sch.buf("co0"), sch.buf("co1")]
        s_view = s_d.rearrange("p (j2 d2) -> p j2 d2", d2=D)
        for j in range(NT):
            sl = j % 2
            sch.dma("sync", xld[:, sl, :], xi[j * 128:(j + 1) * 128, :],
                    reads=[xi_buf], writes=[b_x[sl]], tag="cxl")
            sch.dma("sync", sld[:, sl, :], s_view[:, j, :],
                    reads=[b_sd], writes=[b_s[sl]], tag="csl")
            sch.op("scalar", (lambda e, s=sl, jj=j: e.activation(
                gt[:, s, :], yacc[:, jj, :], AF.Silu)),
                   reads=[b_yacc], writes=[b_g[sl]], tag="silu")
            sch.op("vector", (lambda e, s=sl: e.tensor_tensor(
                mt[:, s, :], sld[:, s, :], gt[:, s, :], OP.mult)),
                   reads=[b_s[sl], b_g[sl]], writes=[b_m[sl]], tag="sg")
            sch.op("vector", (lambda e, s=sl: e.tensor_tensor(
                ot[:, s, :], xld[:, s, :], mt[:, s, :], OP.add)),
                   reads=[b_x[sl], b_m[sl]], writes=[b_o[sl]], tag="xadd")
            sch.dma("sync", xo[j * 128:(j + 1) * 128, :], ot[:, sl, :],
                    reads=[b_o[sl]], writes=[xo_buf], tag="cst")



def emit_layer1(env, xi, xi_buf, xo, xo_buf):
    sch = env.sch
    nc = env.nc
    W = env.W
    with sch.scope():
        wbc, b_wbc = bcast_load(env, W["L1_attn_ln_w"], D, "lnw")
        bbc, b_bbc = bcast_load(env, W["L1_attn_ln_b"], D, "lnb")
        tmk, b_tmk = fm_load(env, W["L1_attn_tmk"], D, "tmk")
        tmv, b_tmv = fm_load(env, W["L1_attn_tmv"], D, "tmv")
        tmr, b_tmr = fm_load(env, W["L1_attn_tmr"], D, "tmr")
        dw, b_dw = fm_load(env, W["L1_attn_decay_w"], D, "dw")
        dec, b_dec = sch.sbuf("dec", [128, ND], F32)
        sch.op("scalar", (lambda e: e.activation(dec[:, :], dw[:, :], AF.Sigmoid)),
               reads=[b_dw], writes=[b_dec], tag="sigdec")
        om = {}
        for nm, (t_, b_) in (("k", (tmk, b_tmk)), ("v", (tmv, b_tmv)),
                             ("r", (tmr, b_tmr))):
            ot_, ob = sch.sbuf("om" + nm, [128, ND], F32)
            sch.op("vector", (lambda e, s=t_, o=ot_: e.tensor_scalar(
                o[:, :], s[:, :], -1.0, 1.0, OP.mult, OP.add)),
                   reads=[b_], writes=[ob], tag="om")
            om[nm] = (ot_, ob)
        gw_s, b_gw = sch.sbuf("gw", [8, D], F32)
        sch.dma("sync", gw_s[:, :], W["L1_ffn_gate_w"], writes=[b_gw])
        rw_fm, b_rw = sch.sbuf("rwfm", [128, ND, 8], F32)
        ps_g, b_psg = sch.psum("psg", [128, 128], F32)
        for c in range(ND):
            sch.op("tensor", (lambda e, cc=c: e.transpose(
                ps_g[:, 0:8], gw_s[0:8, cc * 128:(cc + 1) * 128], env.ident_f32[0:8, 0:8])),
                   reads=[b_gw, env.b_ident_f32], writes=[b_psg], tag="gwtr")
            sch.op("scalar", (lambda e, cc=c: e.copy(rw_fm[:, cc, :],
                                                     ps_g[:, 0:8])),
                   reads=[b_psg], writes=[b_rw], tag="gwev")
        gbias_bc, b_gbias = bcast_load(env, W["L1_ffn_gate_bias"], 8, "gb")
        s_dram = nc.dram_tensor(sch.u("s_d"), [128, NT * D], BF16).ap()
        b_sdram = sch.buf("s_d")
        logits_pre = sch.sbuf("logits", [128, NT, 8], F32)

        with sch.scope():
            mixk, b_mixk = sch.sbuf("mixk", [128, ND, T], BF16)
            mixv, b_mixv = sch.sbuf("mixv", [128, ND, T], BF16)

            with sch.scope():
                mixr, b_mixr = sch.sbuf("mixr", [128, ND, T], BF16)

                with sch.scope():
                    outs = emit_ln_phase(
                        env, xi, xi_buf,
                        {"xn_fm": "dram", "xn_tm": "dram", "xnl_fm": "sbuf",
                         "logits": "sbuf"},
                        inner_wb=(wbc, b_wbc, bbc, b_bbc),
                        router=(rw_fm, b_rw, 8),
                        prealloc={"logits": logits_pre})
                    xnl_fm, b_xnlfm = outs["xnl_fm"]
                    env.l1_xnfm_d = outs["xn_fm"]
                    env.l1_xntm_d = outs["xn_tm"]
                    env.l1_logits = outs["logits"]

                    cc_in = nc.dram_tensor(sch.u("cc1i"), [128, ND], F32).ap()
                    cc_out = nc.dram_tensor(sch.u("cc1o"), [256, ND], F32).ap()
                    b_cci, b_cco = sch.buf("cc1i"), sch.buf("cc1o")
                    tail_s, b_tails = sch.sbuf("tails", [128, ND], F32)
                    sch.op("vector", (lambda e: e.tensor_copy(
                        tail_s[:, :],
                        xnl_fm[:, :, T - 1:T].rearrange("p c o -> p (c o)"))),
                           reads=[b_xnlfm], writes=[b_tails], tag="tailcp")
                    sch.dma("sync", cc_in, tail_s[:, :], reads=[b_tails],
                            writes=[b_cci])
                    sch.cc("AllGather", OP.bypass, PAIRS, [cc_in], [cc_out],
                           reads=[b_cci], writes=[b_cco])
                    prev0, b_prev0 = sch.sbuf("prev0", [128, ND], F32)
                    sch.dma("sync", prev0[:, :], cc_out[0:128, :],
                            reads=[b_cco], writes=[b_prev0])
                    prev0b, b_prev0b = sch.sbuf("prev0b", [128, ND], BF16)
                    sch.op("vector", (lambda e: e.tensor_scalar(
                        prev0[:, :], prev0[:, :], env.hbc[:, 0:1], None,
                        OP.mult)),
                           reads=[b_prev0, env.b_hbc], writes=[b_prev0],
                           tag="hmask")
                    sch.op("vector", (lambda e: e.tensor_copy(
                        prev0b[:, :], prev0[:, :])),
                           reads=[b_prev0], writes=[b_prev0b])

                    mixa, b_mixa = sch.sbuf("mixa", [128, T], BF16)

                    def emit_mix(dst, b_dst, tmx, b_tmx, omx, b_omx, tag):
                        for c in range(ND):
                            sch.op("vector", (lambda e, cc=c: e.tensor_scalar(
                                mixa[:, :], xnl_fm[:, cc, :],
                                tmx[:, cc:cc + 1], None, OP.mult)),
                                   reads=[b_xnlfm, b_tmx], writes=[b_mixa],
                                   tag="mxa" + tag)
                            sch.op("vector", (lambda e, cc=c, d=dst:
                                              e.scalar_tensor_tensor(
                                d[:, cc, 1:T], xnl_fm[:, cc, 0:T - 1],
                                omx[:, cc:cc + 1], mixa[:, 1:T],
                                OP.mult, OP.add)),
                                   reads=[b_xnlfm, b_omx, b_mixa],
                                   writes=[b_dst], tag="mxb" + tag)
                            sch.op("vector", (lambda e, cc=c, d=dst:
                                              e.scalar_tensor_tensor(
                                d[:, cc, 0:1], prev0b[:, cc:cc + 1],
                                omx[:, cc:cc + 1], mixa[:, 0:1],
                                OP.mult, OP.add)),
                                   reads=[b_prev0b, b_omx, b_mixa],
                                   writes=[b_dst], tag="mxc" + tag)

                    emit_mix(mixr, b_mixr, tmr, b_tmr, om["r"][0], om["r"][1], "r")
                    emit_mix(mixk, b_mixk, tmk, b_tmk, om["k"][0], om["k"][1], "k")
                    emit_mix(mixv, b_mixv, tmv, b_tmv, om["v"][0], om["v"][1], "v")

                r_tm, b_rtm = sch.sbuf("rtm", [128, NT, D], BF16, up=1,
                                        side="right")
                env.l1_rtm = (r_tm, b_rtm)
                with sch.scope():
                    WrT, b_WrT = w_transpose(env, W["L1_attn_Wr"], D, D, "wr")
                    ps_r, b_psr = sch.psum("psr", [128, 512], F32)
                    for j in range(NT):
                        for half in range(2):
                            for dcc in range(ND):
                                sch.op("tensor", (lambda e, jj=j, h=half,
                                                  d_=dcc: e.matmul(
                                    ps_r[:, :],
                                    mixr[:, d_, jj * 128:(jj + 1) * 128],
                                    WrT[:, d_, h * 512:(h + 1) * 512],
                                    start=(d_ == 0), stop=(d_ == ND - 1))),
                                       reads=[b_mixr, b_WrT], writes=[b_psr],
                                       tag="rmm")
                            sch.op("scalar", (lambda e, jj=j, h=half:
                                              e.activation(
                                r_tm[:, jj, h * 512:(h + 1) * 512], ps_r[:, :],
                                AF.Sigmoid)),
                                   reads=[b_psr], writes=[b_rtm], tag="rsig")
            r_tm, b_rtm = env.l1_rtm

            S_bf, b_Sbf = sch.sbuf("Sbf", [128, ND, T], BF16)
            with sch.scope():
                WkT, b_WkT = w_transpose(env, W["L1_attn_Wk"], D, D, "wk")
                WvT, b_WvT = w_transpose(env, W["L1_attn_Wv"], D, D, "wv")
                kc, b_kc = sch.sbuf("kc", [128, 512], F32)
                kvc, b_kvc = sch.sbuf("kvc", [128, T], F32)
                dbc, b_dbc = sch.sbuf("dbc", [128, T], F32)
                stail, b_stail = sch.sbuf("stail", [128, ND], F32)
                ps_k, b_psk = sch.psum("psk", [128, 512], F32)
                ps_v, b_psv = sch.psum("psv", [128, 512], F32)
                for c in range(ND):
                    sch.op("gpsimd", (lambda e: e.memset(dbc[:, :], 1.0)),
                           writes=[b_dbc], tag="dmem")
                    sch.op("vector", (lambda e, cc=c: e.tensor_scalar(
                        dbc[:, :], dbc[:, :], dec[:, cc:cc + 1], None,
                        OP.mult)),
                           reads=[b_dec, b_dbc], writes=[b_dbc], tag="dbc")
                    for tc in range(4):
                        for d_ in range(ND):
                            sch.op("tensor", (lambda e, cc=c, t_=tc, dd=d_:
                                              e.matmul(
                                ps_k[:, :], WkT[:, dd, cc * 128:(cc + 1) * 128],
                                mixk[:, dd, t_ * 512:(t_ + 1) * 512],
                                start=(dd == 0), stop=(dd == ND - 1))),
                                   reads=[b_WkT, b_mixk], writes=[b_psk],
                                   tag="kmm")
                        sch.op("scalar", (lambda e, t_=tc: e.copy(
                            kc[:, :], ps_k[:, :])),
                               reads=[b_psk], writes=[b_kc], tag="kev")
                        for d_ in range(ND):
                            sch.op("tensor", (lambda e, cc=c, t_=tc, dd=d_:
                                              e.matmul(
                                ps_v[:, :], WvT[:, dd, cc * 128:(cc + 1) * 128],
                                mixv[:, dd, t_ * 512:(t_ + 1) * 512],
                                start=(dd == 0), stop=(dd == ND - 1))),
                                   reads=[b_WvT, b_mixv], writes=[b_psv],
                                   tag="vmm")
                        sch.op("vector", (lambda e, t_=tc: e.tensor_tensor(
                            kvc[:, t_ * 512:(t_ + 1) * 512], kc[:, :],
                            ps_v[:, :], OP.mult)),
                               reads=[b_kc, b_psv], writes=[b_kvc], tag="kv")
                    sch.op("vector", (lambda e, cc=c: e.tensor_tensor_scan(
                        S_bf[:, cc, :], dbc[:, :], kvc[:, :], 0.0, OP.mult,
                        OP.add)),
                           reads=[b_dbc, b_kvc], writes=[b_Sbf], tag="scan")
                    sch.op("vector", (lambda e, cc=c: e.tensor_copy(
                        stail[:, cc:cc + 1], S_bf[:, cc, T - 1:T])),
                           reads=[b_Sbf], writes=[b_stail], tag="stail")

                cc2_in = nc.dram_tensor(sch.u("cc2i"), [128, ND], F32).ap()
                cc2_out = nc.dram_tensor(sch.u("cc2o"), [256, ND], F32).ap()
                b_c2i, b_c2o = sch.buf("cc2i"), sch.buf("cc2o")
                sch.dma("sync", cc2_in, stail[:, :], reads=[b_stail],
                        writes=[b_c2i])
                sch.cc("AllGather", OP.bypass, PAIRS, [cc2_in], [cc2_out],
                       reads=[b_c2i], writes=[b_c2o])
                shalf, b_shalf = sch.sbuf("shalf", [128, ND], F32)
                sch.dma("sync", shalf[:, :], cc2_out[0:128, :], reads=[b_c2o],
                        writes=[b_shalf])
                sch.op("vector", (lambda e: e.tensor_scalar(
                    shalf[:, :], shalf[:, :], env.hbc[:, 0:1], None, OP.mult)),
                       reads=[b_shalf, env.b_hbc], writes=[b_shalf], tag="hm2")
                zbuf, b_z = sch.sbuf("zbuf", [128, T], F32)
                corrb, b_corrb = sch.sbuf("corrb", [128, T], BF16)
                sch.op("gpsimd", (lambda e: e.memset(zbuf[:, :], 0.0)),
                       writes=[b_z], tag="zmem")
                for c in range(ND):
                    sch.op("gpsimd", (lambda e: e.memset(dbc[:, :], 1.0)),
                           writes=[b_dbc], tag="dmem2")
                    sch.op("vector", (lambda e, cc=c: e.tensor_scalar(
                        dbc[:, :], dbc[:, :], dec[:, cc:cc + 1], None,
                        OP.mult)),
                           reads=[b_dec, b_dbc], writes=[b_dbc], tag="dbc2")
                    sch.op("vector", (lambda e, cc=c: e.tensor_tensor(
                        zbuf[:, 0:1], shalf[:, cc:cc + 1], dec[:, cc:cc + 1],
                        OP.mult)),
                           reads=[b_shalf, b_dec], writes=[b_z], tag="zset")
                    sch.op("vector", (lambda e: e.tensor_tensor_scan(
                        kvc[:, :], dbc[:, :], zbuf[:, :], 0.0, OP.mult,
                        OP.add)),
                           reads=[b_dbc, b_z], writes=[b_kvc], tag="cscan")
                    sch.op("vector", (lambda e: e.tensor_copy(
                        corrb[:, :], kvc[:, :])),
                           reads=[b_kvc], writes=[b_corrb], tag="ccast")
                    sch.op("vector", (lambda e, cc=c: e.tensor_tensor(
                        S_bf[:, cc, :], S_bf[:, cc, :], corrb[:, :], OP.add)),
                           reads=[b_corrb, b_Sbf], writes=[b_Sbf], tag="cadd")

            with sch.scope():
                WoT, b_WoT = w_transpose(env, W["L1_attn_Wo"], D, D, "wo")
                xn_ld, _ = sch.sbuf("xnld", [128, 2, D], BF16)
                b_xnld = [sch.buf("xnl0"), sch.buf("xnl1")]
                s_sb, _ = sch.sbuf("ssb", [128, 2, D], BF16)
                b_ssb = [sch.buf("ssb0"), sch.buf("ssb1")]
                mtmp, _ = sch.sbuf("mtmp", [128, 512], F32)
                b_mtmp = sch.buf("mtmp")
                ps_o, b_pso = sch.psum("pso", [128, 512], F32)
                xnview = env.l1_xntm_d[0].rearrange("p (j2 d2) -> p j2 d2", d2=D)
                sview = s_dram.rearrange("p (j2 d2) -> p j2 d2", d2=D)
                for j in range(NT):
                    sl = j % 2
                    sch.dma("sync", xn_ld[:, sl, :], xnview[:, j, :],
                            reads=[env.l1_xntm_d[1]], writes=[b_xnld[sl]],
                            tag="xnl")
                    for half in range(2):
                        for ec in range(ND):
                            sch.op("tensor", (lambda e, jj=j, h=half, e_=ec:
                                              e.matmul(
                                ps_o[:, :],
                                S_bf[:, e_, jj * 128:(jj + 1) * 128],
                                WoT[:, e_, h * 512:(h + 1) * 512],
                                start=(e_ == 0), stop=(e_ == ND - 1))),
                                   reads=[b_Sbf, b_WoT], writes=[b_pso],
                                   tag="omm")
                        sch.op("vector", (lambda e, jj=j, h=half:
                                          e.tensor_tensor(
                            mtmp[:, :], r_tm[:, jj, h * 512:(h + 1) * 512],
                            ps_o[:, :], OP.mult)),
                               reads=[b_rtm, b_pso], writes=[b_mtmp],
                               tag="rmul")
                        sch.op("vector", (lambda e, jj=j, h=half, s=sl:
                                          e.tensor_tensor(
                            s_sb[:, s, h * 512:(h + 1) * 512],
                            xn_ld[:, s, h * 512:(h + 1) * 512], mtmp[:, :],
                            OP.add)),
                               reads=[b_xnld[sl], b_mtmp], writes=[b_ssb[sl]],
                               tag="sadd")
                    sch.dma("sync", sview[:, j, :], s_sb[:, sl, :],
                            reads=[b_ssb[sl]], writes=[b_sdram], tag="sst")

        yacc, b_yacc = sch.sbuf("yacc", [128, NT, D], BF16)
        with sch.scope():
            logits, b_logits = env.l1_logits
            gates, b_gates = sch.sbuf("gates", [128, NT, 8], F32)
            gs, _ = sch.sbuf("gs", [128, 12], F32)
            b_gs = sch.buf("gs")
            sco, b_sco = sch.sbuf("sco", [128, 8], F32)
            sbv, b_sbv = sch.sbuf("sbv", [128, 8], F32)
            mk1, b_mk1 = sch.sbuf("mk1", [128, 8], F32)
            for j in range(NT):
                sch.op("vector", (lambda e, jj=j: e.tensor_reduce(
                    gs[:, 0:1], logits[:, jj, :], AX.X, OP.max)),
                       reads=[b_logits], writes=[b_gs], tag="gmax")
                sch.op("vector", (lambda e: e.tensor_scalar(
                    gs[:, 1:2], gs[:, 0:1], -1.0, None, OP.mult)),
                       reads=[b_gs], writes=[b_gs])
                sch.op("scalar", (lambda e, jj=j: e.activation(
                    sco[:, :], logits[:, jj, :], AF.Exp, bias=gs[:, 1:2],
                    accum_out=gs[:, 2:3])),
                       reads=[b_logits, b_gs], writes=[b_sco, b_gs], tag="gexp")
                sch.op("vector", (lambda e: e.reciprocal(gs[:, 3:4],
                                                         gs[:, 2:3])),
                       reads=[b_gs], writes=[b_gs])
                sch.op("vector", (lambda e: e.tensor_scalar(
                    sco[:, :], sco[:, :], gs[:, 3:4], None, OP.mult)),
                       reads=[b_gs, b_sco], writes=[b_sco], tag="gsc")
                sch.op("vector", (lambda e: e.tensor_tensor(
                    sbv[:, :], sco[:, :], gbias_bc[:, 0:8], OP.add)),
                       reads=[b_sco, b_gbias], writes=[b_sbv])
                sch.op("vector", (lambda e: e.tensor_reduce(
                    gs[:, 4:5], sbv[:, :], AX.X, OP.max)),
                       reads=[b_sbv], writes=[b_gs])
                sch.op("vector", (lambda e: e.tensor_scalar(
                    mk1[:, :], sbv[:, :], gs[:, 4:5], None, OP.is_ge)),
                       reads=[b_sbv, b_gs], writes=[b_mk1], tag="gm1")
                sch.op("vector", (lambda e: e.scalar_tensor_tensor(
                    sbv[:, :], mk1[:, :], -BIG, sbv[:, :], OP.mult, OP.add)),
                       reads=[b_mk1, b_sbv], writes=[b_sbv], tag="gdrop")
                sch.op("vector", (lambda e: e.tensor_reduce(
                    gs[:, 5:6], sbv[:, :], AX.X, OP.max)),
                       reads=[b_sbv], writes=[b_gs])
                sch.op("vector", (lambda e: e.tensor_scalar(
                    sbv[:, :], sbv[:, :], gs[:, 5:6], None, OP.is_ge)),
                       reads=[b_gs, b_sbv], writes=[b_sbv], tag="gm2")
                sch.op("vector", (lambda e: e.tensor_tensor(
                    mk1[:, :], mk1[:, :], sbv[:, :], OP.add)),
                       reads=[b_sbv, b_mk1], writes=[b_mk1])
                sch.op("vector", (lambda e, jj=j: e.tensor_tensor(
                    gates[:, jj, :], mk1[:, :], sco[:, :], OP.mult)),
                       reads=[b_mk1, b_sco], writes=[b_gates], tag="gate")

            xn_fm, b_xnfm = sch.sbuf("xnfm", [128, ND, T], BF16)
            sch.dma("sync", xn_fm[:, :, :],
                    env.l1_xnfm_d[0].rearrange("p (c t2) -> p c t2", t2=T),
                    reads=[env.l1_xnfm_d[1]], writes=[b_xnfm], tag="xnfl")

            hh, b_hh = sch.sbuf("hh", [128, 8, T], BF16)
            w1T, bw1 = sch.sbuf("w1T", [128, 8, 1024], BF16)
            w3T, bw3 = sch.sbuf("w3T", [128, 8, 1024], BF16)
            w2T, bw2 = sch.sbuf("w2T", [128, 8, 1024], BF16)
            asil, b_asil = sch.sbuf("asil", [128, 512], BF16)
            ps_1, b_ps1 = sch.psum("ps1", [128, 512], F32)
            ps_3, b_ps3 = sch.psum("ps3", [128, 512], F32)
            ps_o2, b_pso2 = sch.psum("pso2", [128, 512], F32)

            def expert_pass(gate_col, first, ns=8):
                for fc in range(ns):
                    for tc in range(4):
                        for d_ in range(ND):
                            sch.op("tensor", (lambda e, f=fc, t_=tc, dd=d_:
                                              e.matmul(
                                ps_1[:, :], w1T[:, dd, f * 128:(f + 1) * 128],
                                xn_fm[:, dd, t_ * 512:(t_ + 1) * 512],
                                start=(dd == 0), stop=(dd == ND - 1))),
                                   reads=[bw1, b_xnfm], writes=[b_ps1],
                                   tag="f1mm")
                        sch.op("scalar", (lambda e: e.activation(
                            asil[:, :], ps_1[:, :], AF.Silu)),
                               reads=[b_ps1], writes=[b_asil], tag="fsil")
                        for d_ in range(ND):
                            sch.op("tensor", (lambda e, f=fc, t_=tc, dd=d_:
                                              e.matmul(
                                ps_3[:, :], w3T[:, dd, f * 128:(f + 1) * 128],
                                xn_fm[:, dd, t_ * 512:(t_ + 1) * 512],
                                start=(dd == 0), stop=(dd == ND - 1))),
                                   reads=[bw3, b_xnfm], writes=[b_ps3],
                                   tag="f3mm")
                        sch.op("vector", (lambda e, f=fc, t_=tc:
                                          e.tensor_tensor(
                            hh[:, f, t_ * 512:(t_ + 1) * 512], asil[:, :],
                            ps_3[:, :], OP.mult)),
                               reads=[b_asil, b_ps3], writes=[b_hh], tag="fmul")
                for j in range(NT):
                    for half in range(2):
                        for fc in range(ns):
                            sch.op("tensor", (lambda e, jj=j, h=half, f=fc:
                                              e.matmul(
                                ps_o2[:, :], hh[:, f, jj * 128:(jj + 1) * 128],
                                w2T[:, f, h * 512:(h + 1) * 512],
                                start=(f == 0), stop=(f == ns - 1))),
                                   reads=[b_hh, bw2], writes=[b_pso2],
                                   tag="f2mm")
                        if first:
                            sch.op("vector", (lambda e, jj=j, h=half,
                                              gc=gate_col: e.tensor_scalar(
                                yacc[:, jj, h * 512:(h + 1) * 512],
                                ps_o2[:, :], gc(jj), None, OP.mult)),
                                   reads=[b_pso2, b_gates], writes=[b_yacc],
                                   tag="facc0")
                        elif gate_col is not None:
                            sch.op("vector", (lambda e, jj=j, h=half,
                                              gc=gate_col:
                                              e.scalar_tensor_tensor(
                                yacc[:, jj, h * 512:(h + 1) * 512],
                                ps_o2[:, :], gc(jj),
                                yacc[:, jj, h * 512:(h + 1) * 512],
                                OP.mult, OP.add)),
                                   reads=[b_pso2, b_gates, b_yacc],
                                   writes=[b_yacc], tag="facc")
                        else:
                            sch.op("vector", (lambda e, jj=j, h=half:
                                              e.tensor_tensor(
                                yacc[:, jj, h * 512:(h + 1) * 512],
                                ps_o2[:, :],
                                yacc[:, jj, h * 512:(h + 1) * 512], OP.add)),
                                   reads=[b_pso2, b_yacc], writes=[b_yacc],
                                   tag="fadd")

            for e_i in range(8):
                w_transpose(env, W["L1_ffn_w1"][e_i:e_i + 1, :, :].rearrange(
                    "a b c -> (a b) c"), D, D, f"w1e{e_i}", dst=(w1T, bw1))
                w_transpose(env, W["L1_ffn_w3"][e_i:e_i + 1, :, :].rearrange(
                    "a b c -> (a b) c"), D, D, f"w3e{e_i}", dst=(w3T, bw3))
                w_transpose(env, W["L1_ffn_w2"][e_i:e_i + 1, :, :].rearrange(
                    "a b c -> (a b) c"), D, D, f"w2e{e_i}", dst=(w2T, bw2))
                gc = (lambda jj, ee=e_i: gates[:, jj, ee:ee + 1])
                expert_pass(gc, first=(e_i == 0))
            for hf in range(2):
                w_transpose(env, W["L1_ffn_sw1"][hf * 1024:(hf + 1) * 1024, :],
                            D, D, f"sw1h{hf}", dst=(w1T, bw1))
                w_transpose(env, W["L1_ffn_sw3"][hf * 1024:(hf + 1) * 1024, :],
                            D, D, f"sw3h{hf}", dst=(w3T, bw3))
                w_transpose(env, W["L1_ffn_sw2"][:, hf * 1024:(hf + 1) * 1024],
                            D, D, f"sw2h{hf}", dst=(w2T, bw2))
                expert_pass(None, first=False)

        emit_combine(env, xi, xi_buf, xo, xo_buf, s_dram, b_sdram, yacc, b_yacc)


def emit_layer2(env, xi, xi_buf, xo, xo_buf):
    sch = env.sch
    nc = env.nc
    W = env.W
    DEC = 0.99
    with sch.scope():
        rw_fm, b_rw = sch.sbuf("rw2", [128, ND, 16], F32)
        sch.dma("sync", rw_fm[:, :, :],
                W["L2_ffn_router_w"].rearrange("(c p) e -> p c e", p=128),
                writes=[b_rw], tag="rwl")
        bkc, b_bkc = fm_load(env, W["L2_attn_bk"], 128, "bk")
        bqc, b_bqc = fm_load(env, W["L2_attn_bq"], 128, "bq")
        bv_bc, b_bvbc = bcast_load(env, W["L2_attn_bv"], 128, "bv")
        bk_bc, b_bkbc = bcast_load(env, W["L2_attn_bk"], 128, "bk2")
        bo_bc, b_bobc = bcast_load(env, W["L2_attn_bo"], D, "bo")
        dtri, b_dtri = sch.sbuf("dtri", [128, 128], BF16)
        sch.dma("sync", dtri[:, :], env.consts_dram["dec_tri_T"], writes=[b_dtri])
        dpow, b_dpow = sch.sbuf("dpow", [128, 1], F32)
        sch.dma("sync", dpow[:, :], env.consts_dram["dec_pow"], writes=[b_dpow])
        dtail, b_dtail = sch.sbuf("dtail", [128, 1], F32)
        sch.dma("sync", dtail[:, :], env.consts_dram["dec_tail"], writes=[b_dtail])
        s_dram = nc.dram_tensor(sch.u("s2_d"), [128, NT * D], BF16).ap()
        b_sdram = sch.buf("s2_d")
        logits_pre = sch.sbuf("logits2", [128, NT, 16], F32)

        xn_fm, b_xnfm = sch.sbuf("xnfm2", [128, ND, T], BF16)
        with sch.scope():
            outs = emit_ln_phase(
                env, xi, xi_buf, {"xn_fm": "sbuf", "logits": "sbuf"},
                router=(rw_fm, b_rw, 16),
                prealloc={"xn_fm": (xn_fm, b_xnfm),
                          "logits": logits_pre})
        logits, b_logits = logits_pre

        with sch.scope():
            WkT, b_WkT = w_transpose(env, W["L2_attn_Wk"], 128, D, "l2wk")
            WvT, b_WvT = w_transpose(env, W["L2_attn_Wv"], 128, D, "l2wv")
            WqT, b_WqT = w_transpose(env, W["L2_attn_Wq"], 128, D, "l2wq")
            WoT, b_WoT = w_transpose(env, W["L2_attn_Wo"], D, 128, "l2wo")
            q_fm, b_qfm = sch.sbuf("qfm", [128, T], BF16)
            k_fm, b_kfm = sch.sbuf("kfm2", [128, T], BF16)
            v_tm, b_vtm = sch.sbuf("vtm", [128, NT, 128], BF16)
            k_tm, b_ktm = sch.sbuf("ktm", [128, NT, 128], BF16)
            out_tm, b_otm = sch.sbuf("otm", [128, NT, 128], BF16)
            ps_f, b_psf = sch.psum("psf2", [128, 512], F32)
            ps_t, b_pst = sch.psum("pst2", [128, 128], F32)
            for (dst, b_dst, wt, b_wt, bias, b_bias) in (
                    (q_fm, b_qfm, WqT, b_WqT, bqc, b_bqc),
                    (k_fm, b_kfm, WkT, b_WkT, bkc, b_bkc)):
                for tc in range(4):
                    for dc in range(ND):
                        sch.op("tensor", (lambda e, t_=tc, d_=dc, w=wt:
                                          e.matmul(
                            ps_f[:, :], w[:, d_, :],
                            xn_fm[:, d_, t_ * 512:(t_ + 1) * 512],
                            start=(d_ == 0), stop=(d_ == ND - 1))),
                               reads=[b_wt, b_xnfm], writes=[b_psf], tag="qkm")
                    sch.op("scalar", (lambda e, t_=tc, d2=dst, bb=bias:
                                      e.activation(
                        d2[:, t_ * 512:(t_ + 1) * 512], ps_f[:, :], AF.Identity,
                        bias=bb[:, 0:1])),
                           reads=[b_psf, b_bias], writes=[b_dst], tag="qkev")
            for (dst, b_dst, wt, b_wt, bbc_t, b_bbc_t) in (
                    (v_tm, b_vtm, WvT, b_WvT, bv_bc, b_bvbc),
                    (k_tm, b_ktm, WkT, b_WkT, bk_bc, b_bkbc)):
                for j in range(NT):
                    for dc in range(ND):
                        sch.op("tensor", (lambda e, jj=j, d_=dc, w=wt:
                                          e.matmul(
                            ps_t[:, :], xn_fm[:, d_, jj * 128:(jj + 1) * 128],
                            w[:, d_, :],
                            start=(d_ == 0), stop=(d_ == ND - 1))),
                               reads=[b_wt, b_xnfm], writes=[b_pst], tag="vkm")
                    sch.op("vector", (lambda e, jj=j, d2=dst, bb=bbc_t:
                                      e.tensor_tensor(
                        d2[:, jj, :], ps_t[:, :], bb[:, 0:128], OP.add)),
                           reads=[b_pst, b_bbc_t], writes=[b_dst], tag="vkev")

            S_T, b_ST = sch.sbuf("ST", [128, 128], F32)
            S_Tb, b_STb = sch.sbuf("STb", [128, 128], BF16)
            kscl, b_kscl = sch.sbuf("kscl", [128, 128], BF16)
            intra, b_intra = sch.sbuf("intra", [128, 128], F32)
            P_sb, b_P = sch.sbuf("Psb", [128, 128], BF16)
            ps_a, b_psa = sch.psum("psa", [128, 128], F32)
            ps_i, b_psi = sch.psum("psi", [128, 128], F32)
            ps_n, b_psn = sch.psum("psn", [128, 128], F32)
            ps_s, b_pss = sch.psum("pss", [128, 128], F32)
            sch.op("gpsimd", (lambda e: e.memset(S_T[:, :], 0.0)),
                   writes=[b_ST], tag="st0")
            for g in range(NT):
                sch.op("vector", (lambda e: e.tensor_copy(S_Tb[:, :], S_T[:, :])),
                       reads=[b_ST], writes=[b_STb], tag="stcast")
                sch.op("tensor", (lambda e, gg=g: e.matmul(
                    ps_a[:, :], k_fm[:, gg * 128:(gg + 1) * 128],
                    q_fm[:, gg * 128:(gg + 1) * 128], start=True, stop=True)),
                       reads=[b_kfm, b_qfm], writes=[b_psa], tag="amm")
                sch.op("vector", (lambda e: e.tensor_tensor(
                    P_sb[:, :], ps_a[:, :], dtri[:, :], OP.mult)),
                       reads=[b_psa, b_dtri], writes=[b_P], tag="pmask")
                sch.op("tensor", (lambda e, gg=g: e.matmul(
                    ps_i[:, :], P_sb[:, :], v_tm[:, gg, :],
                    start=True, stop=True)),
                       reads=[b_P, b_vtm], writes=[b_psi], tag="imm")
                sch.op("tensor", (lambda e, gg=g: e.matmul(
                    ps_n[:, :], q_fm[:, gg * 128:(gg + 1) * 128], S_Tb[:, :],
                    start=True, stop=True)),
                       reads=[b_qfm, b_STb], writes=[b_psn], tag="nmm")
                sch.op("scalar", (lambda e: e.copy(intra[:, :], ps_i[:, :])),
                       reads=[b_psi], writes=[b_intra], tag="iev")
                sch.op("vector", (lambda e, gg=g: e.scalar_tensor_tensor(
                    out_tm[:, gg, :], ps_n[:, :], dpow[:, 0:1], intra[:, :],
                    OP.mult, OP.add)),
                       reads=[b_psn, b_dpow, b_intra], writes=[b_otm],
                       tag="oev")
                sch.op("vector", (lambda e, gg=g: e.tensor_scalar(
                    kscl[:, :], k_tm[:, gg, :], dtail[:, 0:1], None, OP.mult)),
                       reads=[b_ktm, b_dtail], writes=[b_kscl], tag="kscl")
                sch.op("tensor", (lambda e, gg=g: e.matmul(
                    ps_s[:, :], kscl[:, :], v_tm[:, gg, :],
                    start=True, stop=True)),
                       reads=[b_kscl, b_vtm], writes=[b_pss], tag="smm")
                sch.op("vector", (lambda e: e.scalar_tensor_tensor(
                    S_T[:, :], S_T[:, :], DEC ** 128, ps_s[:, :],
                    OP.mult, OP.add)),
                       reads=[b_pss, b_ST], writes=[b_ST], tag="supd")

            cc_in = nc.dram_tensor(sch.u("cc3i"), [128, 128], F32).ap()
            cc_out = nc.dram_tensor(sch.u("cc3o"), [256, 128], F32).ap()
            b_c3i, b_c3o = sch.buf("cc3i"), sch.buf("cc3o")
            sch.dma("sync", cc_in, S_T[:, :], reads=[b_ST], writes=[b_c3i])
            sch.cc("AllGather", OP.bypass, PAIRS, [cc_in], [cc_out],
                   reads=[b_c3i], writes=[b_c3o])
            shalf, b_shalf = sch.sbuf("shalf2", [128, 128], F32)
            sch.dma("sync", shalf[:, :], cc_out[0:128, :], reads=[b_c3o],
                    writes=[b_shalf])
            sch.op("vector", (lambda e: e.tensor_scalar(
                shalf[:, :], shalf[:, :], env.hbc[:, 0:1], None, OP.mult)),
                   reads=[b_shalf, env.b_hbc], writes=[b_shalf], tag="hm3")
            sg_b, b_sgb = sch.sbuf("sgb", [128, 128], BF16)
            for g in range(NT):
                sc = float(DEC ** (128 * g))
                sch.op("vector", (lambda e, s=sc: e.tensor_scalar(
                    sg_b[:, :], shalf[:, :], s, None, OP.mult)),
                       reads=[b_shalf], writes=[b_sgb], tag="sgc")
                sch.op("tensor", (lambda e, gg=g: e.matmul(
                    ps_n[:, :], q_fm[:, gg * 128:(gg + 1) * 128], sg_b[:, :],
                    start=True, stop=True)),
                       reads=[b_qfm, b_sgb], writes=[b_psn], tag="cmm")
                sch.op("vector", (lambda e, gg=g: e.scalar_tensor_tensor(
                    out_tm[:, gg, :], ps_n[:, :], dpow[:, 0:1],
                    out_tm[:, gg, :], OP.mult, OP.add)),
                       reads=[b_psn, b_dpow, b_otm], writes=[b_otm],
                       tag="cadd2")

            out_fmT, b_ofmT = sch.sbuf("ofmT", [128, 128], BF16)
            ps_y, b_psy = sch.psum("psy", [128, 512], F32)
            ps_tr3, b_ptr3 = sch.psum("ptr3", [128, 128], BF16)
            ssb, b_ssb = sch.sbuf("ssb2", [128, D], BF16)
            sview = s_dram.rearrange("p (j2 d2) -> p j2 d2", d2=D)
            for j in range(NT):
                sch.op("tensor", (lambda e, jj=j: e.transpose(
                    ps_tr3[:, :], out_tm[:, jj, :], env.ident_bf[:, :])),
                       reads=[b_otm, env.b_ident_bf], writes=[b_ptr3],
                       tag="otr")
                sch.op("vector", (lambda e: e.tensor_copy(out_fmT[:, :],
                                                          ps_tr3[:, :])),
                       reads=[b_ptr3], writes=[b_ofmT], tag="otev")
                for half in range(2):
                    sch.op("tensor", (lambda e, h=half: e.matmul(
                        ps_y[:, :], out_fmT[:, :],
                        WoT[:, 0, h * 512:(h + 1) * 512],
                        start=True, stop=True)),
                           reads=[b_ofmT, b_WoT], writes=[b_psy], tag="ymm")
                    sch.op("vector", (lambda e, h=half: e.tensor_tensor(
                        ssb[:, h * 512:(h + 1) * 512], ps_y[:, :],
                        bo_bc[:, h * 512:(h + 1) * 512], OP.add)),
                           reads=[b_psy, b_bobc], writes=[b_ssb], tag="yev")
                sch.dma("sync", sview[:, j, :], ssb[:, :], reads=[b_ssb],
                        writes=[b_sdram], tag="sst2")

        yacc, b_yacc = sch.sbuf("yacc2", [128, NT, D], BF16)
        wsig, b_wsig = sch.sbuf("wsig", [128, NT, 1], F32)
        with sch.scope():
            mask_t, b_mask = sch.sbuf("mask2", [128, NT, 16], BF16)
            mask_f, b_maskf = sch.sbuf("mask2f", [128, NT, 16], F32)
            gs, _ = sch.sbuf("gs2", [128, 4], F32)
            b_gs = sch.buf("gs2")
            for j in range(NT):
                sch.op("vector", (lambda e, jj=j: e.tensor_reduce(
                    gs[:, 0:1], logits[:, jj, :], AX.X, OP.max)),
                       reads=[b_logits], writes=[b_gs], tag="g2max")
                sch.op("scalar", (lambda e, jj=j: e.activation(
                    wsig[:, jj, :], gs[:, 0:1], AF.Sigmoid)),
                       reads=[b_gs], writes=[b_wsig], tag="g2sig")
                sch.op("vector", (lambda e, jj=j: e.tensor_scalar(
                    mask_f[:, jj, :], logits[:, jj, :], gs[:, 0:1], None,
                    OP.is_ge)),
                       reads=[b_logits, b_gs], writes=[b_maskf], tag="g2m")
                sch.op("vector", (lambda e, jj=j: e.tensor_copy(
                    mask_t[:, jj, :], mask_f[:, jj, :])),
                       reads=[b_maskf], writes=[b_mask], tag="g2mc")
            mask_fmT, b_mfmT = sch.sbuf("mfmT", [16, NT, 128], BF16)
            ps_mt, b_pmt = sch.psum("pmt", [16, 128], BF16)
            for j in range(NT):
                sch.op("tensor", (lambda e, jj=j: e.transpose(
                    ps_mt[:, :], mask_t[:, jj, :], env.ident_bf[:, :])),
                       reads=[b_mask, env.b_ident_bf], writes=[b_pmt],
                       tag="mtr")
                sch.op("vector", (lambda e, jj=j: e.tensor_copy(
                    mask_fmT[:, jj, :], ps_mt[:, :])),
                       reads=[b_pmt], writes=[b_mfmT], tag="mtev")
            b2s, b_b2s = sch.sbuf("b2s", [16, D], BF16)
            b2f, b_b2f = sch.sbuf("b2f", [16, D], F32)
            sch.dma("sync", b2f[:, :], W["L2_ffn_b2"], writes=[b_b2f])
            sch.op("vector", (lambda e: e.tensor_copy(b2s[:, :], b2f[:, :])),
                   reads=[b_b2f], writes=[b_b2s], tag="b2c")
            ps_o2, b_pso2 = sch.psum("pso2b", [128, 2, 512], F32)
            for j in range(NT):
                for half in range(2):
                    sch.op("tensor", (lambda e, jj=j, h=half: e.matmul(
                        ps_o2[:, h, :], mask_fmT[:, jj, :],
                        b2s[:, h * 512:(h + 1) * 512],
                        start=True, stop=True)),
                           reads=[b_mfmT, b_b2s], writes=[b_pso2], tag="b2mm")
                    sch.op("vector", (lambda e, jj=j, h=half: e.tensor_copy(
                        yacc[:, jj, h * 512:(h + 1) * 512], ps_o2[:, h, :])),
                           reads=[b_pso2], writes=[b_yacc], tag="b2ev")

            w1s, b_w1s = sch.sbuf("w1s", [128, 8, 2048], BF16)
            w2s, b_w2s = sch.sbuf("w2s", [128, 16, 1024], BF16)
            wst, b_wst = sch.sbuf("wst", [128, 2, 2048], F32)
            b1c, b_b1c = sch.sbuf("b1c", [128, 16], F32)
            hhg, b_hhg = sch.sbuf("hhg2", [128, 16, 512], BF16)
            ps_h, _ = sch.psum("psh2", [128, 2, 512], F32)
            b_psh2 = [sch.buf("psh2a"), sch.buf("psh2b")]
            for e_i in range(16):
                w1d = W["L2_ffn_w1"][e_i:e_i + 1, :, :].rearrange(
                    "a b c -> (a b) c")
                w2d = W["L2_ffn_w2"][e_i:e_i + 1, :, :].rearrange(
                    "a b c -> (a b) c")
                b1d = W["L2_ffn_b1"][e_i:e_i + 1, :].rearrange("a b -> (a b)")
                sch.dma("sync", b1c[:, :],
                        b1d.rearrange("(c p) -> p c", p=128),
                        writes=[b_b1c], tag="b1l",
                        allow_slow_non_contiguous=True)
                for dc in range(ND):
                    sl = dc % 2
                    sch.dma("sync", wst[:, sl, :],
                            w1d[dc * 128:(dc + 1) * 128, :],
                            writes=[b_wst], tag="w1l")
                    sch.op("vector", (lambda e, d_=dc, s=sl: e.tensor_copy(
                        w1s[:, d_, :], wst[:, s, :])),
                           reads=[b_wst], writes=[b_w1s], tag="w1c")
                for fc in range(16):
                    sl = fc % 2
                    sch.dma("sync", wst[:, sl, 0:1024],
                            w2d[fc * 128:(fc + 1) * 128, :],
                            writes=[b_wst], tag="w2l")
                    sch.op("vector", (lambda e, f_=fc, s=sl: e.tensor_copy(
                        w2s[:, f_, :], wst[:, s, 0:1024])),
                           reads=[b_wst], writes=[b_w2s], tag="w2c")
                for tc in range(4):
                    for fc in range(16):
                        sl = fc % 2
                        for dc in range(ND):
                            sch.op("tensor", (lambda e, f_=fc, t_=tc, d_=dc,
                                              s=sl: e.matmul(
                                ps_h[:, s, :],
                                w1s[:, d_, f_ * 128:(f_ + 1) * 128],
                                xn_fm[:, d_, t_ * 512:(t_ + 1) * 512],
                                start=(d_ == 0), stop=(d_ == ND - 1))),
                                   reads=[b_w1s, b_xnfm], writes=[b_psh2[sl]],
                                   tag="h1mm")
                        sch.op("scalar", (lambda e, f_=fc, s=sl: e.activation(
                            hhg[:, f_, :], ps_h[:, s, :], AF.Gelu,
                            bias=b1c[:, f_:f_ + 1])),
                               reads=[b_psh2[sl], b_b1c], writes=[b_hhg],
                               tag="hgelu")
                    for jr in range(4):
                        j = tc * 4 + jr
                        for half in range(2):
                            for fc in range(16):
                                sch.op("tensor", (lambda e, jj=jr, h=half,
                                                  f_=fc: e.matmul(
                                    ps_o2[:, h, :],
                                    hhg[:, f_, jj * 128:(jj + 1) * 128],
                                    w2s[:, f_, h * 512:(h + 1) * 512],
                                    start=(f_ == 0), stop=(f_ == 15))),
                                       reads=[b_hhg, b_w2s], writes=[b_pso2],
                                       tag="h2mm")
                            sch.op("vector", (lambda e, jj=j, h=half,
                                              ee=e_i:
                                              e.scalar_tensor_tensor(
                                yacc[:, jj, h * 512:(h + 1) * 512],
                                ps_o2[:, h, :],
                                mask_f[:, jj, ee:ee + 1],
                                yacc[:, jj, h * 512:(h + 1) * 512],
                                OP.mult, OP.add)),
                                   reads=[b_pso2, b_maskf, b_yacc],
                                   writes=[b_yacc], tag="hacc")

        emit_combine2(env, xi, xi_buf, xo, xo_buf, s_dram, b_sdram,
                      yacc, b_yacc, wsig, b_wsig)


def emit_combine2(env, xi, xi_buf, xo, xo_buf, s_d, b_sd, yacc, b_yacc,
                  wsig, b_wsig):
    sch = env.sch
    with sch.scope():
        xld, _ = sch.sbuf("xld2", [128, 2, D], F32)
        sld, _ = sch.sbuf("sld2", [128, 2, D], BF16)
        gt, _ = sch.sbuf("gt2", [128, 2, D], BF16)
        mt, _ = sch.sbuf("mt2", [128, 2, D], F32)
        ot, _ = sch.sbuf("ot2", [128, 2, D], F32)
        b_x = [sch.buf("cx0"), sch.buf("cx1")]
        b_s = [sch.buf("cs0"), sch.buf("cs1")]
        b_g = [sch.buf("cg0"), sch.buf("cg1")]
        b_m = [sch.buf("cm0"), sch.buf("cm1")]
        b_o = [sch.buf("co0"), sch.buf("co1")]
        s_view = s_d.rearrange("p (j2 d2) -> p j2 d2", d2=D)
        for j in range(NT):
            sl = j % 2
            sch.dma("sync", xld[:, sl, :], xi[j * 128:(j + 1) * 128, :],
                    reads=[xi_buf], writes=[b_x[sl]], tag="cxl")
            sch.dma("sync", sld[:, sl, :], s_view[:, j, :],
                    reads=[b_sd], writes=[b_s[sl]], tag="csl")
            sch.op("scalar", (lambda e, s=sl, jj=j: e.activation(
                gt[:, s, :], yacc[:, jj, :], AF.Silu,
                scale=wsig[:, jj, 0:1])),
                   reads=[b_yacc, b_wsig], writes=[b_g[sl]], tag="silu2")
            sch.op("vector", (lambda e, s=sl: e.tensor_tensor(
                mt[:, s, :], sld[:, s, :], gt[:, s, :], OP.mult)),
                   reads=[b_s[sl], b_g[sl]], writes=[b_m[sl]], tag="sg2")
            sch.op("vector", (lambda e, s=sl: e.tensor_tensor(
                ot[:, s, :], xld[:, s, :], mt[:, s, :], OP.add)),
                   reads=[b_x[sl], b_m[sl]], writes=[b_o[sl]], tag="xadd2")
            sch.dma("sync", xo[j * 128:(j + 1) * 128, :], ot[:, sl, :],
                    reads=[b_o[sl]], writes=[xo_buf], tag="cst2")


def emit_quant(env, x_ap, b_x, out_ap, b_out, tmp, width, tag):
    sch = env.sch
    nb = width // 16
    r1, b_r1 = tmp["r1"]
    sc, b_sc = tmp["sc"]
    inv, b_inv = tmp["inv"]
    rr, b_rr = tmp["rr"]
    aa, b_aa = tmp["aa"]
    n1, b_n1 = tmp["n1"]
    n2, b_n2 = tmp["n2"]
    sg, b_sg = tmp["sg"]
    sch.op("vector", (lambda e: e.scalar_tensor_tensor(
        aa[:, 0:width], x_ap, -1.0, x_ap, OP.mult, OP.max)),
           reads=[b_x], writes=[b_aa], tag="qt0" + tag)
    a16 = aa[:, 0:width].rearrange("p (b k) -> p b k", k=16)
    sch.op("vector", (lambda e: e.tensor_tensor(
        rr[:, 0:width // 2].rearrange("p (b k) -> p b k", k=8),
        a16[:, :, 0:8], a16[:, :, 8:16], OP.max)),
           reads=[b_aa], writes=[b_rr], tag="qt1" + tag)
    r8 = rr[:, 0:width // 2].rearrange("p (b k) -> p b k", k=8)
    sch.op("vector", (lambda e: e.tensor_tensor(
        aa[:, 0:width // 4].rearrange("p (b k) -> p b k", k=4),
        r8[:, :, 0:4], r8[:, :, 4:8], OP.max)),
           reads=[b_rr], writes=[b_aa], tag="qt2" + tag)
    r4 = aa[:, 0:width // 4].rearrange("p (b k) -> p b k", k=4)
    sch.op("vector", (lambda e: e.tensor_tensor(
        rr[:, 0:width // 8].rearrange("p (b k) -> p b k", k=2),
        r4[:, :, 0:2], r4[:, :, 2:4], OP.max)),
           reads=[b_aa], writes=[b_rr], tag="qt3" + tag)
    r2 = rr[:, 0:width // 8].rearrange("p (b k) -> p b k", k=2)
    sch.op("vector", (lambda e: e.tensor_tensor(
        r1[:, 0:nb].unsqueeze(2),
        r2[:, :, 0:1], r2[:, :, 1:2], OP.max)),
           reads=[b_rr], writes=[b_r1], tag="qt4" + tag)
    sch.op("vector", (lambda e: e.tensor_scalar(
        sc[:, 0:nb], r1[:, 0:nb], 1.0 / 6.0, 1.0e-12, OP.mult, OP.max)),
           reads=[b_r1], writes=[b_sc], tag="qsc" + tag)
    sch.op("vector", (lambda e: e.reciprocal(inv[:, 0:nb], sc[:, 0:nb])),
           reads=[b_sc], writes=[b_inv], tag="qinv" + tag)
    inv3 = inv[:, 0:nb].unsqueeze(2).broadcast_to([128, nb, 16])
    sch.op("vector", (lambda e: e.tensor_tensor(
        rr[:, 0:width].rearrange("p (b k) -> p b k", k=16), x_ap.rearrange(
            "p (b k) -> p b k", k=16), inv3, OP.mult)),
           reads=[b_x, b_inv], writes=[b_rr], tag="qr" + tag)
    sch.op("vector", (lambda e: e.scalar_tensor_tensor(
        aa[:, 0:width], rr[:, 0:width], -1.0, rr[:, 0:width],
        OP.mult, OP.max)),
           reads=[b_rr], writes=[b_aa], tag="qa" + tag)
    sch.op("vector", (lambda e: e.tensor_scalar(
        sg[:, 0:width], rr[:, 0:width], 0.0, None, OP.is_ge)),
           reads=[b_rr], writes=[b_sg], tag="qsg" + tag)
    sch.op("vector", (lambda e: e.tensor_scalar(
        sg[:, 0:width], sg[:, 0:width], 2.0, -1.0, OP.mult, OP.add)),
           reads=[b_sg], writes=[b_sg], tag="qsg2" + tag)
    sch.op("vector", (lambda e: e.tensor_scalar(
        n1[:, 0:width], aa[:, 0:width], 0.25, None, OP.is_gt)),
           reads=[b_aa], writes=[b_n1], tag="qc1" + tag)
    for thr in (0.75, 1.25, 1.75):
        sch.op("vector", (lambda e, t_=thr: e.scalar_tensor_tensor(
            n1[:, 0:width], aa[:, 0:width], t_, n1[:, 0:width],
            OP.is_gt, OP.add)),
               reads=[b_aa, b_n1], writes=[b_n1], tag="qcc" + tag)
    sch.op("vector", (lambda e: e.tensor_scalar(
        n2[:, 0:width], aa[:, 0:width], 2.5, None, OP.is_gt)),
           reads=[b_aa], writes=[b_n2], tag="qc2" + tag)
    sch.op("vector", (lambda e: e.scalar_tensor_tensor(
        n2[:, 0:width], aa[:, 0:width], 3.5, n2[:, 0:width],
        OP.is_gt, OP.add)),
           reads=[b_aa, b_n2], writes=[b_n2], tag="qc3" + tag)
    sch.op("vector", (lambda e: e.scalar_tensor_tensor(
        n2[:, 0:width], n1[:, 0:width], 0.5, n2[:, 0:width],
        OP.mult, OP.add)),
           reads=[b_n1, b_n2], writes=[b_n2], tag="qq1" + tag)
    sch.op("vector", (lambda e: e.tensor_scalar(
        n1[:, 0:width], aa[:, 0:width], 5.0, 2.0, OP.is_gt, OP.mult)),
           reads=[b_aa], writes=[b_n1], tag="qq2" + tag)
    sch.op("vector", (lambda e: e.tensor_tensor(
        n2[:, 0:width], n2[:, 0:width], n1[:, 0:width], OP.add)),
           reads=[b_n1, b_n2], writes=[b_n2], tag="qq3" + tag)
    sch.op("vector", (lambda e: e.tensor_tensor(
        n2[:, 0:width], n2[:, 0:width], sg[:, 0:width], OP.mult)),
           reads=[b_sg, b_n2], writes=[b_n2], tag="qs1" + tag)
    sc3 = sc[:, 0:nb].unsqueeze(2).broadcast_to([128, nb, 16])
    sch.op("vector", (lambda e: e.tensor_tensor(
        out_ap.rearrange("p (b k) -> p b k", k=16),
        n2[:, 0:width].rearrange("p (b k) -> p b k", k=16), sc3, OP.mult)),
           reads=[b_sc, b_n2], writes=[b_out], tag="qs2" + tag)


L3_STOP = None
RMS_TRUNC = 99


def emit_layer3(env, xi, xi_buf, xo, xo_buf):
    sch = env.sch
    nc = env.nc
    W = env.W
    EPS3 = 1.1920929e-07
    NH = 8
    SFULL = 4096

    def _stop(name):
        if L3_STOP == name:
            for j in range(NT):
                sch.dma("sync", xo[j * 128:(j + 1) * 128, :],
                        xi[j * 128:(j + 1) * 128, :], reads=[xi_buf],
                        writes=[xo_buf], tag="dbgcp")
            return True
        return False
    with sch.scope():
        if _stop("nothing"):
            return
        s_dram = nc.dram_tensor(sch.u("s3_d"), [128, NT * D], BF16).ap()
        b_sdram = sch.buf("s3_d")
        hbar, b_hbar = sch.sbuf("hbar", [128, 1], F32)
        sch.op("vector", (lambda e: e.tensor_scalar(
            hbar[:, :], env.hbc[:, :], -1.0, 1.0, OP.mult, OP.add)),
               reads=[env.b_hbc], writes=[b_hbar], tag="hbar")
        bo_bc, b_bobc = bcast_load(env, W["L3_attn_b_o"], D, "bo3")

        ccq_in = nc.dram_tensor(sch.u("ccqi"), [128, 24 * T], BF16).ap()
        ccq_out = nc.dram_tensor(sch.u("ccqo"), [256, 24 * T], BF16).ap()
        b_cqi, b_cqo = sch.buf("ccqi"), sch.buf("ccqo")
        with sch.scope():
          rms_fm, b_rmsfm = sch.sbuf("rmsfm", [128, ND, T], BF16)
          with sch.scope():
            outs = emit_ln_phase(env, xi, xi_buf, {"xn_fm": "dram"},
                                 rms_out=(None if L3_STOP == "ln0" else
                                          (rms_fm, b_rmsfm, EPS3)))
            env.l3_xnfm_d = outs["xn_fm"]
          if _stop("ln0"):
              return
          if _stop("ln"):
              return

          qkv_fm, b_qkv = sch.sbuf("qkvfm", [128, 24, T], BF16)
          if True:
            with sch.scope():
              wq_view = W["L3_attn_W_qkv"].rearrange(
                  "(h d s) c -> s (h d) c", s=3, d=64)
              bq_view = W["L3_attn_b_qkv"].rearrange(
                  "(h d s) -> s (h d)", s=3, d=64)
              ps_p, b_psp = sch.psum("psp3", [128, 512], F32)
              for part in range(3):
                  WT, b_WT = w_transpose(env, wq_view[part], 1024, D,
                                         f"l3w{part}")
                  bcol, b_bcol = fm_load(env, bq_view[part], 1024, f"l3b{part}")
                  for oc in range(8):
                      for tc in range(4):
                          for dc in range(ND):
                              sch.op("tensor", (lambda e, o=oc, t_=tc, d_=dc,
                                                w=WT: e.matmul(
                                  ps_p[:, :], w[:, d_, o * 128:(o + 1) * 128],
                                  rms_fm[:, d_, t_ * 512:(t_ + 1) * 512],
                                  start=(d_ == 0), stop=(d_ == ND - 1))),
                                     reads=[b_WT, b_rmsfm], writes=[b_psp],
                                     tag="qkvm")
                          sch.op("scalar", (lambda e, o=oc, t_=tc, pp=part,
                                            bb=bcol: e.activation(
                              qkv_fm[:, pp * 8 + o, t_ * 512:(t_ + 1) * 512],
                              ps_p[:, :], AF.Identity, bias=bb[:, o:o + 1])),
                                 reads=[b_psp, b_bcol], writes=[b_qkv],
                                 tag="qkvev")

          if _stop("proj"):
              return
          sch.dma("sync", ccq_in,
                  qkv_fm[:, :, :].rearrange("p c t -> p (c t)"),
                  reads=[b_qkv], writes=[b_cqi], tag="qpub")
          if _stop("pub"):
              return
        sch.cc("AllGather", OP.bypass, PAIRS, [ccq_in], [ccq_out],
               reads=[b_cqi], writes=[b_cqo])
        if _stop("qkv"):
            return

        with sch.scope():
          Qf, b_Qf = sch.sbuf("Qf", [128, 4, SFULL], BF16)
          Kf, b_Kf = sch.sbuf("Kf", [128, 4, SFULL], BF16)
          V_tm, b_Vtm = sch.sbuf("Vtm3", [128, 32, 512], BF16)
          z_fm, b_zfm = sch.sbuf("zfm", [128, 4, SFULL], BF16)
          if True:
            with sch.scope():
              Vf, b_Vf = sch.sbuf("Vf", [128, 4, SFULL], BF16)
              candA, b_cA = sch.sbuf("candA", [128, SFULL], BF16)
              candB, b_cB = sch.sbuf("candB", [128, SFULL], BF16)
              gview = ccq_out.rearrange("(s p) (c t) -> p s c t", p=128, t=T)
              for (dst, b_dst, base) in ((Qf, b_Qf, 0), (Kf, b_Kf, 8),
                                         (Vf, b_Vf, 16)):
                  for cq in range(4):
                      for (cand, b_c, coff) in ((candA, b_cA, 0),
                                                (candB, b_cB, 4)):
                          for slot in range(2):
                              sch.dma("sync", cand[:, slot * T:(slot + 1) * T],
                                      gview[:, slot, base + coff + cq, :],
                                      reads=[b_cqo], writes=[b_c], tag="qkvl")
                      sch.op("vector", (lambda e, d=dst, c=cq: e.tensor_scalar(
                          d[:, c, :], candA[:, :], hbar[:, 0:1], None, OP.mult)),
                             reads=[b_cA, b_hbar], writes=[b_dst], tag="qsel")
                      sch.op("vector", (lambda e, d=dst, c=cq:
                                        e.scalar_tensor_tensor(
                          d[:, c, :], candB[:, :], env.hbc[:, 0:1], d[:, c, :],
                          OP.mult, OP.add)),
                             reads=[b_cB, env.b_hbc, b_dst], writes=[b_dst],
                             tag="qsel2")
              ps_tr, b_ptr = sch.psum("ptr4", [128, 128], BF16)
              for cq in range(4):
                  for jt in range(32):
                      sch.op("tensor", (lambda e, c=cq, j=jt: e.transpose(
                          ps_tr[:, :], Vf[:, c, j * 128:(j + 1) * 128],
                          env.ident_bf[:, :])),
                             reads=[b_Vf, env.b_ident_bf], writes=[b_ptr],
                             tag="vtr")
                      sch.op("vector", (lambda e, c=cq, j=jt: e.tensor_copy(
                          V_tm[:, j, c * 128:(c + 1) * 128], ps_tr[:, :])),
                             reads=[b_ptr], writes=[b_Vtm], tag="vtev")

          if _stop("assemble"):
              return
          nslots = sum(4 * qc + 4 for qc in range(8))
          Pd = nc.dram_tensor(sch.u("Pspill"), [128, nslots * 512], BF16).ap()
          b_Pd = sch.buf("Pspill")
          Pview = Pd.rearrange("p (s q) -> p s q", q=512)
          with sch.scope():
              iota_t, b_iota = sch.sbuf("iota3", [128, 512], F32)
              sch.op("gpsimd", (lambda e: e.iota(
                  iota_t[:, :], [[1, 512]], channel_multiplier=-1,
                  allow_small_or_imprecise_dtypes=True)),
                     writes=[b_iota], tag="iota")
              mbs, b_mbs = sch.sbuf("mbs3", [128, 4, 512], F32)
              for ix in range(4):
                  sch.op("vector", (lambda e, i_=ix: e.tensor_scalar(
                      mbs[:, i_, :], iota_t[:, :], float(i_ * 128), -1.0e9,
                      OP.is_lt, OP.mult)),
                         reads=[b_iota], writes=[b_mbs], tag="mkb")
              mrun, b_mrun = sch.sbuf("mrun", [128, 32], F32)
              srun, b_srun = sch.sbuf("srun", [128, 32], F32)
              mhist, b_mhist = sch.sbuf("mhist", [128, 32, 8], F32)
              st3, b_st3 = sch.sbuf("st3", [128, 8], F32)
              Ssb, b_Ssb = sch.sbuf("Ssb", [128, 512], F32)
              Pt, b_Pt = sch.sbuf("Pt", [128, 8, 512], BF16)
              mb, b_mb = sch.sbuf("mb3", [128, 512], F32)
              ps_s, b_pss = sch.psum("pss3", [128, 512], F32)
              ps_z, b_psz = sch.psum("psz3", [64, 512], F32)
              vsc, b_vsc = sch.sbuf("vsc", [128, 8, 64], BF16)
              env.l3_vsc_bufs = [sch.buf(f"vscs{s}") for s in range(8)]
              env.l3_pt_bufs = [sch.buf(f"ptbs{s}") for s in range(8)]
              env.l3_slot_free = {}
              sinv, b_sinv = sch.sbuf("sinv", [128, 32], F32)

              for hh in range(NH):
                  cq = hh // 2
                  r0 = (hh % 2) * 64
                  slot = 0
                  if hh > 0:
                      sch.fence([b_Pt, b_psz, b_zfm], tag="hfence")
                  sch.op("gpsimd", (lambda e: e.memset(mrun[:, :], -3.0e38)),
                         writes=[b_mrun], tag="m0")
                  sch.op("gpsimd", (lambda e: e.memset(srun[:, :], 0.0)),
                         writes=[b_srun], tag="s0")
                  for qc in range(8):
                      for kt in range(4 * qc + 4):
                          sch.op("tensor", (lambda e, c=cq, r=r0, k=kt, q=qc:
                                            e.matmul(
                              ps_s[:, :],
                              Kf[r:r + 64, c, k * 128:(k + 1) * 128],
                              Qf[r:r + 64, c, q * 512:(q + 1) * 512],
                              start=True, stop=True)),
                                 reads=[b_Kf, b_Qf], writes=[b_pss], tag="smm")
                          diag = (kt * 128 + 127 > qc * 512)
                          if diag:
                              ix = kt - 4 * qc
                              sch.op("vector", (lambda e, i_=ix:
                                                e.tensor_tensor(
                                  Ssb[:, :], ps_s[:, :], mbs[:, i_, :],
                                  OP.add)),
                                     reads=[b_pss, b_mbs], writes=[b_Ssb],
                                     tag="smask")
                              sv, b_sv = Ssb, b_Ssb
                          else:
                              sv, b_sv = ps_s, b_pss
                          sch.op("vector", (lambda e, s=sv: e.tensor_reduce(
                              st3[:, 0:1], s[:, :], AX.X, OP.max)),
                                 reads=[b_sv], writes=[b_st3], tag="rmax")
                          sch.op("vector", (lambda e, k=kt: e.tensor_tensor(
                              st3[:, 1:2], st3[:, 0:1], mrun[:, k:k + 1],
                              OP.max)),
                                 reads=[b_st3, b_mrun], writes=[b_st3],
                                 tag="mnew")
                          sch.op("vector", (lambda e: e.tensor_scalar(
                              st3[:, 2:3], st3[:, 1:2], -1.0, None, OP.mult)),
                                 reads=[b_st3], writes=[b_st3], tag="mneg")
                          sch.op("scalar", (lambda e, s=sv: e.activation(
                              Pt[:, 0, :], s[:, :], AF.Exp, bias=st3[:, 2:3],
                              accum_out=st3[:, 3:4])),
                                 reads=[b_sv, b_st3], writes=[b_Pt, b_st3],
                                 tag="pexp")
                          sch.op("vector", (lambda e, k=kt: e.tensor_tensor(
                              st3[:, 4:5], mrun[:, k:k + 1], st3[:, 2:3],
                              OP.add)),
                                 reads=[b_mrun, b_st3], writes=[b_st3], tag="dm")
                          sch.op("scalar", (lambda e: e.activation(
                              st3[:, 5:6], st3[:, 4:5], AF.Exp)),
                                 reads=[b_st3], writes=[b_st3], tag="edm")
                          sch.op("vector", (lambda e, k=kt:
                                            e.scalar_tensor_tensor(
                              srun[:, k:k + 1], srun[:, k:k + 1], st3[:, 5:6],
                              st3[:, 3:4], OP.mult, OP.add)),
                                 reads=[b_st3, b_srun], writes=[b_srun],
                                 tag="supd3")
                          sch.op("vector", (lambda e, k=kt: e.tensor_copy(
                              mrun[:, k:k + 1], st3[:, 1:2])),
                                 reads=[b_st3], writes=[b_mrun], tag="mupd")
                          sch.op("vector", (lambda e, k=kt, q=qc: e.tensor_copy(
                              mhist[:, k, q:q + 1], st3[:, 1:2])),
                                 reads=[b_st3], writes=[b_mhist], tag="mh")
                          sch.dma("sync", Pview[:, slot, :], Pt[:, 0, :],
                                  reads=[b_Pt], writes=[b_Pd], tag="pspill")
                          slot += 1
                  sch.op("vector", (lambda e: e.reciprocal(sinv[:, :],
                                                           srun[:, :])),
                         reads=[b_srun], writes=[b_sinv], tag="sinv")
                  slot2 = 0
                  for qc in range(8):
                      nkt = 4 * qc + 4
                      nsub = (nkt + 7) // 8
                      for sg in range(nsub):
                          k0 = sg * 8
                          k1 = min(k0 + 8, nkt)
                          for kt in range(k0, k1):
                              sl = kt - k0
                              sch.op("vector", (lambda e, k=kt, q=qc:
                                                e.tensor_tensor(
                                  st3[:, 6:7], mhist[:, k, q:q + 1],
                                  mrun[:, k:k + 1], OP.subtract)),
                                     reads=[b_mhist, b_mrun], writes=[b_st3],
                                     tag="fsub")
                              sch.op("scalar", (lambda e: e.activation(
                                  st3[:, 7:8], st3[:, 6:7], AF.Exp)),
                                     reads=[b_st3], writes=[b_st3],
                                     tag="fexp")
                              sch.op("vector", (lambda e, k=kt:
                                                e.tensor_tensor(
                                  st3[:, 6:7], st3[:, 7:8], sinv[:, k:k + 1],
                                  OP.mult)),
                                     reads=[b_st3, b_sinv], writes=[b_st3],
                                     tag="fmul")
                              aft = env.l3_slot_free.get(("v", sl))
                              bv = sch.buf(f"vscf{sl}")
                              sch.op("vector", (lambda e, c=cq, r=r0, k=kt,
                                                s=sl: e.tensor_scalar(
                                  vsc[:, s, :],
                                  V_tm[:, k, c * 128 + r:c * 128 + r + 64],
                                  st3[:, 6:7], None, OP.mult)),
                                     reads=[b_Vtm, b_st3],
                                     writes=[bv],
                                     after=(aft,) if aft else (),
                                     tag="vscl")
                              aft = env.l3_slot_free.get(("p", sl))
                              bp = sch.buf(f"ptf{sl}")
                              sch.dma("sync", Pt[:, sl, :],
                                      Pview[:, slot2, :],
                                      reads=[b_Pd],
                                      writes=[bp],
                                      after=(aft,) if aft else (),
                                      tag="pload")
                              sch.op("tensor", (lambda e, s=sl,
                                                fi=(kt == k0),
                                                la=(kt == k1 - 1): e.matmul(
                                  ps_z[:, :], vsc[:, s, :], Pt[:, s, :],
                                  start=fi, stop=la)),
                                     reads=[bv, bp],
                                     writes=[b_psz], tag="zmm")
                              slot2 += 1
                          if sg == 0:
                              ev = sch.op("vector", (lambda e, c=cq, r=r0,
                                                     q=qc: e.tensor_copy(
                                  z_fm[r:r + 64, c, q * 512:(q + 1) * 512],
                                  ps_z[0:64, :])),
                                          reads=[b_psz], writes=[b_zfm],
                                          tag="zev")
                          else:
                              ev = sch.op("vector", (lambda e, c=cq, r=r0,
                                                     q=qc: e.tensor_tensor(
                                  z_fm[r:r + 64, c, q * 512:(q + 1) * 512],
                                  ps_z[0:64, :],
                                  z_fm[r:r + 64, c, q * 512:(q + 1) * 512],
                                  OP.add)),
                                          reads=[b_psz, b_zfm],
                                          writes=[b_zfm], tag="zacc")
                          for kt in range(k0, k1):
                              sl = kt - k0
                              env.l3_slot_free[("v", sl)] = ev
                              env.l3_slot_free[("p", sl)] = ev


          if _stop("pass12"):
              return
          ccz_in = nc.dram_tensor(sch.u("cczi"), [128, 4 * SFULL], BF16).ap()
          ccz_out = nc.dram_tensor(sch.u("cczo"), [256, 4 * SFULL], BF16).ap()
          b_czi, b_czo = sch.buf("cczi"), sch.buf("cczo")
          sch.dma("sync", ccz_in, z_fm[:, :, :].rearrange("p c t -> p (c t)"),
                  reads=[b_zfm], writes=[b_czi], tag="zpub")
          sch.cc("AllGather", OP.bypass, PAIRS, [ccz_in], [ccz_out],
                 reads=[b_czi], writes=[b_czo])
          zfull, b_zfull = sch.sbuf("zfull", [128, ND, T], BF16)
          with sch.scope():
              zA, b_zA = sch.sbuf("zA", [128, T], BF16)
              zB, b_zB = sch.sbuf("zB", [128, T], BF16)
              zv = ccz_out.rearrange("(s p) (c t) -> p s c t", p=128, t=SFULL)
              for c_abs in range(8):
                  slot = c_abs // 4
                  csrc = c_abs % 4
                  sch.dma("sync", zA[:, :], zv[:, slot, csrc, 0:T],
                          reads=[b_czo], writes=[b_zA], tag="zla")
                  sch.dma("sync", zB[:, :], zv[:, slot, csrc, T:SFULL],
                          reads=[b_czo], writes=[b_zB], tag="zlb")
                  sch.op("vector", (lambda e, c=c_abs: e.tensor_scalar(
                      zfull[:, c, :], zA[:, :], hbar[:, 0:1], None, OP.mult)),
                         reads=[b_zA, b_hbar], writes=[b_zfull], tag="zselA")
                  sch.op("vector", (lambda e, c=c_abs: e.scalar_tensor_tensor(
                      zfull[:, c, :], zB[:, :], env.hbc[:, 0:1],
                      zfull[:, c, :], OP.mult, OP.add)),
                         reads=[b_zB, env.b_hbc, b_zfull], writes=[b_zfull],
                         tag="zselB")

          with sch.scope():
              WoT, b_WoT = w_transpose(env, W["L3_attn_W_o"], D, D, "l3wo")
              ps_y, b_psy = sch.psum("psy3", [128, 512], F32)
              ssb, b_ssb = sch.sbuf("ssb3", [128, D], BF16)
              sview = s_dram.rearrange("p (j2 d2) -> p j2 d2", d2=D)
              for j in range(NT):
                  for half in range(2):
                      for c in range(ND):
                          sch.op("tensor", (lambda e, jj=j, h=half, cc=c:
                                            e.matmul(
                              ps_y[:, :], zfull[:, cc, jj * 128:(jj + 1) * 128],
                              WoT[:, cc, h * 512:(h + 1) * 512],
                              start=(cc == 0), stop=(cc == ND - 1))),
                                 reads=[b_zfull, b_WoT], writes=[b_psy],
                                 tag="ymm3")
                      sch.op("vector", (lambda e, h=half: e.tensor_tensor(
                          ssb[:, h * 512:(h + 1) * 512], ps_y[:, :],
                          bo_bc[:, h * 512:(h + 1) * 512], OP.add)),
                             reads=[b_psy, b_bobc], writes=[b_ssb], tag="yev3")
                  sch.dma("sync", sview[:, j, :], ssb[:, :], reads=[b_ssb],
                          writes=[b_sdram], tag="sst3")

        if _stop("attn"):
            return
        IH = 2816
        IHH = 1408
        IHHC = IHH // 128
        yacc, b_yacc = sch.sbuf("yacc3", [128, NT, D], BF16)
        with sch.scope():
            tmp = {}
            for nm, shape in (("r1", [128, IHH // 16]), ("sc", [128, IHH // 16]),
                              ("inv", [128, IHH // 16]), ("rr", [128, IHH]),
                              ("aa", [128, IHH]), ("n1", [128, IHH]),
                              ("n2", [128, IHH]), ("sg", [128, IHH])):
                tmp[nm] = sch.sbuf("q" + nm, shape, F32)

            def quant_weight(w_view, n_out, n_in, tag, dst):
                noc = n_out // 128
                nic = n_in // 128
                wt, b_wt = dst
                wrow_f, b_wrow = qw_stage
                wq_f, b_wq = qw_stage2
                ps, b_ps = qw_ps
                wrow = wrow_f[:, 0:n_in]
                wq = wq_f[:, 0:n_in]
                for i in range(noc):
                    sch.dma("sync", wrow[:, :],
                            w_view[i * 128:(i + 1) * 128, :],
                            writes=[b_wrow], tag="ql" + tag)
                    emit_quant(env, wrow[:, :], b_wrow, wq[:, :], b_wq, tmp,
                               n_in, tag)
                    for gi in range((nic + 7) // 8):
                        cg = gi * 8
                        n = min(8, nic - cg)
                        for c in range(cg, cg + n):
                            sch.op("tensor", (lambda e, cc=c, c0=cg:
                                              e.transpose(
                                ps[:, (cc - c0) * 128:(cc - c0 + 1) * 128],
                                wq[:, cc * 128:(cc + 1) * 128],
                                env.ident_bf[:, :])),
                                   reads=[b_wq, env.b_ident_bf],
                                   writes=[b_ps], tag="qtr" + tag)
                        sch.op("scalar", (lambda e, c0=cg, n_=n, ii=i: e.copy(
                            wt[:, c0:c0 + n_, ii * 128:(ii + 1) * 128],
                            ps[:, 0:n_ * 128].rearrange(
                                "p (c q2) -> p c q2", q2=128))),
                               reads=[b_ps], writes=[b_wt], tag="qev" + tag)
                return wt, b_wt

            xq_fm, b_xqfm = sch.sbuf("xqfm", [128, ND, T], BF16)
            with sch.scope():
                xnt_l, b_xntl = sch.sbuf("xntl", [128, D], BF16)
                xqt, b_xqt = sch.sbuf("xqt", [128, D], BF16)
                ps_q, b_psq = sch.psum("psq3", [128, 1024], BF16)
                fmcol, b_fmcol = sch.sbuf("fmcol", [128, ND, 128], BF16)
                xnview = env.l3_xnfm_d[0].rearrange("p (c t2) -> p c t2", t2=T)
                for j in range(NT):
                    sch.dma("sync", fmcol[:, :, :],
                            xnview[:, :, j * 128:(j + 1) * 128],
                            reads=[env.l3_xnfm_d[1]], writes=[b_fmcol],
                            tag="xfl")
                    for c in range(ND):
                        sch.op("tensor", (lambda e, cc=c: e.transpose(
                            ps_q[:, cc * 128:(cc + 1) * 128], fmcol[:, cc, :],
                            env.ident_bf[:, :])),
                               reads=[b_fmcol, env.b_ident_bf],
                               writes=[b_psq], tag="xtr3")
                    sch.op("scalar", (lambda e: e.copy(xnt_l[:, :],
                                                       ps_q[:, :])),
                           reads=[b_psq], writes=[b_xntl], tag="xtev3")
                    emit_quant(env, xnt_l[:, :], b_xntl, xqt[:, :], b_xqt,
                               tmp, D, "xq")
                    for c in range(ND):
                        sch.op("tensor", (lambda e, cc=c: e.transpose(
                            ps_q[:, cc * 128:(cc + 1) * 128],
                            xqt[:, cc * 128:(cc + 1) * 128],
                            env.ident_bf[:, :])),
                               reads=[b_xqt, env.b_ident_bf], writes=[b_psq],
                               tag="xqtr")
                    sch.op("scalar", (lambda e, jj=j: e.copy(
                        xq_fm[:, :, jj * 128:(jj + 1) * 128],
                        ps_q[:, :].rearrange("p (c q2) -> p c q2", q2=128))),
                           reads=[b_psq], writes=[b_xqfm], tag="xqev")

            fchunks = [(0, 512), (512, 512), (1024, 384)]
            qw_stage = sch.sbuf("qwst", [128, IHH], F32)
            qw_stage2 = sch.sbuf("qwst2", [128, IHH], BF16)
            qw_ps = sch.psum("qwps", [128, 1024], BF16)
            wgq, b_wgq = sch.sbuf("wgq", [128, ND, IHH], BF16)
            wuq, b_wuq = sch.sbuf("wuq", [128, ND, IHH], BF16)
            wdq, b_wdq = sch.sbuf("wdq", [128, IHHC, D], BF16)
            gsil, b_gsil = sch.sbuf("gsil", [128, 512], BF16)
            htile, b_htile = sch.sbuf("htile", [128, IHH], BF16)
            hqt, b_hqt = sch.sbuf("hqt", [128, IHH], BF16)
            hqT, b_hqT = sch.sbuf("hqT", [128, IHHC, 128], BF16)
            ps_g, b_psg3 = sch.psum("psg3", [128, 512], F32)
            ps_u, b_psu3 = sch.psum("psu3", [128, 512], F32)
            ps_o, b_pso3 = sch.psum("pso3", [128, 2, 512], F32)
            ps_ht, b_psht = sch.psum("psht", [128, 1024], BF16)
            for ihalf in range(2):
                if True:
                    quant_weight(
                        W["L3_ffn_w_gate"][ihalf * IHH:(ihalf + 1) * IHH, :],
                        IHH, D, f"wg{ihalf}", (wgq, b_wgq))
                    quant_weight(
                        W["L3_ffn_w_up"][ihalf * IHH:(ihalf + 1) * IHH, :],
                        IHH, D, f"wu{ihalf}", (wuq, b_wuq))
                    quant_weight(
                        W["L3_ffn_w_down"][:, ihalf * IHH:(ihalf + 1) * IHH],
                        D, IHH, f"wd{ihalf}", (wdq, b_wdq))
                    for j in range(NT):
                        for (f0, fw) in fchunks:
                            for dc in range(ND):
                                sch.op("tensor", (lambda e, jj=j, d_=dc,
                                                  ff=f0, w=fw: e.matmul(
                                    ps_g[:, 0:w],
                                    xq_fm[:, d_, jj * 128:(jj + 1) * 128],
                                    wgq[:, d_, ff:ff + w],
                                    start=(d_ == 0), stop=(d_ == ND - 1))),
                                       reads=[b_xqfm, b_wgq],
                                       writes=[b_psg3], tag="gmm3")
                            sch.op("scalar", (lambda e, w=fw: e.activation(
                                gsil[:, 0:w], ps_g[:, 0:w], AF.Silu)),
                                   reads=[b_psg3], writes=[b_gsil],
                                   tag="gsil3")
                            for dc in range(ND):
                                sch.op("tensor", (lambda e, jj=j, d_=dc,
                                                  ff=f0, w=fw: e.matmul(
                                    ps_u[:, 0:w],
                                    xq_fm[:, d_, jj * 128:(jj + 1) * 128],
                                    wuq[:, d_, ff:ff + w],
                                    start=(d_ == 0), stop=(d_ == ND - 1))),
                                       reads=[b_xqfm, b_wuq],
                                       writes=[b_psu3], tag="umm3")
                            sch.op("vector", (lambda e, ff=f0, w=fw:
                                              e.tensor_tensor(
                                htile[:, ff:ff + w], gsil[:, 0:w],
                                ps_u[:, 0:w], OP.mult)),
                                   reads=[b_gsil, b_psu3], writes=[b_htile],
                                   tag="hmul3")
                        emit_quant(env, htile[:, :], b_htile, hqt[:, :],
                                   b_hqt, tmp, IHH, "hq")
                        for gi in range(2):
                            cg = gi * 8
                            n = min(8, IHHC - cg)
                            for c in range(cg, cg + n):
                                sch.op("tensor", (lambda e, cc=c, c0=cg:
                                                  e.transpose(
                                    ps_ht[:, (cc - c0) * 128:
                                          (cc - c0 + 1) * 128],
                                    hqt[:, cc * 128:(cc + 1) * 128],
                                    env.ident_bf[:, :])),
                                       reads=[b_hqt, env.b_ident_bf],
                                       writes=[b_psht], tag="htr")
                            sch.op("scalar", (lambda e, c0=cg, n_=n: e.copy(
                                hqT[:, c0:c0 + n_, :],
                                ps_ht[:, 0:n_ * 128].rearrange(
                                    "p (c q2) -> p c q2", q2=128))),
                                   reads=[b_psht], writes=[b_hqT],
                                   tag="htev")
                        for half in range(2):
                            for ic in range(IHHC):
                                sch.op("tensor", (lambda e, h=half, i_=ic:
                                                  e.matmul(
                                    ps_o[:, h, :], hqT[:, i_, :],
                                    wdq[:, i_, h * 512:(h + 1) * 512],
                                    start=(i_ == 0), stop=(i_ == IHHC - 1))),
                                       reads=[b_hqT, b_wdq],
                                       writes=[b_pso3], tag="dmm3")
                            if ihalf == 0:
                                sch.op("vector", (lambda e, jj=j, h=half:
                                                  e.tensor_copy(
                                    yacc[:, jj, h * 512:(h + 1) * 512],
                                    ps_o[:, h, :])),
                                       reads=[b_pso3], writes=[b_yacc],
                                       tag="dev3")
                            else:
                                sch.op("vector", (lambda e, jj=j, h=half:
                                                  e.tensor_tensor(
                                    yacc[:, jj, h * 512:(h + 1) * 512],
                                    ps_o[:, h, :],
                                    yacc[:, jj, h * 512:(h + 1) * 512],
                                    OP.add)),
                                       reads=[b_pso3, b_yacc],
                                       writes=[b_yacc], tag="dacc3")

        emit_combine(env, xi, xi_buf, xo, xo_buf, s_dram, b_sdram, yacc, b_yacc)


def emit_layer2(env, xi, xi_buf, xo, xo_buf):
    sch = env.sch
    nc = env.nc
    W = env.W
    DEC = 0.99
    with sch.scope():
        rw_fm, b_rw = sch.sbuf("rw2", [128, ND, 16], F32)
        sch.dma("sync", rw_fm[:, :, :],
                W["L2_ffn_router_w"].rearrange("(c p) e -> p c e", p=128),
                writes=[b_rw], tag="rwl")
        bkc, b_bkc = fm_load(env, W["L2_attn_bk"], 128, "bk")
        bqc, b_bqc = fm_load(env, W["L2_attn_bq"], 128, "bq")
        bv_bc, b_bvbc = bcast_load(env, W["L2_attn_bv"], 128, "bv")
        bk_bc, b_bkbc = bcast_load(env, W["L2_attn_bk"], 128, "bk2")
        bo_bc, b_bobc = bcast_load(env, W["L2_attn_bo"], D, "bo")
        dtri, b_dtri = sch.sbuf("dtri", [128, 128], BF16)
        sch.dma("sync", dtri[:, :], env.consts_dram["dec_tri_T"], writes=[b_dtri])
        dpow, b_dpow = sch.sbuf("dpow", [128, 1], F32)
        sch.dma("sync", dpow[:, :], env.consts_dram["dec_pow"], writes=[b_dpow])
        dtail, b_dtail = sch.sbuf("dtail", [128, 1], F32)
        sch.dma("sync", dtail[:, :], env.consts_dram["dec_tail"], writes=[b_dtail])
        s_dram = nc.dram_tensor(sch.u("s2_d"), [128, NT * D], BF16).ap()
        b_sdram = sch.buf("s2_d")
        logits_pre = sch.sbuf("logits2", [128, NT, 16], F32)

        xn_fm, b_xnfm = sch.sbuf("xnfm2", [128, ND, T], BF16)
        with sch.scope():
            outs = emit_ln_phase(
                env, xi, xi_buf, {"xn_fm": "sbuf", "logits": "sbuf"},
                router=(rw_fm, b_rw, 16),
                prealloc={"xn_fm": (xn_fm, b_xnfm),
                          "logits": logits_pre})
        logits, b_logits = logits_pre

        with sch.scope():
            WkT, b_WkT = w_transpose(env, W["L2_attn_Wk"], 128, D, "l2wk")
            WvT, b_WvT = w_transpose(env, W["L2_attn_Wv"], 128, D, "l2wv")
            WqT, b_WqT = w_transpose(env, W["L2_attn_Wq"], 128, D, "l2wq")
            WoT, b_WoT = w_transpose(env, W["L2_attn_Wo"], D, 128, "l2wo")
            q_fm, b_qfm = sch.sbuf("qfm", [128, T], BF16)
            k_fm, b_kfm = sch.sbuf("kfm2", [128, T], BF16)
            v_tm, b_vtm = sch.sbuf("vtm", [128, NT, 128], BF16)
            k_tm, b_ktm = sch.sbuf("ktm", [128, NT, 128], BF16)
            out_tm, b_otm = sch.sbuf("otm", [128, NT, 128], BF16)
            ps_f, b_psf = sch.psum("psf2", [128, 512], F32)
            ps_t, b_pst = sch.psum("pst2", [128, 128], F32)
            for (dst, b_dst, wt, b_wt, bias, b_bias) in (
                    (q_fm, b_qfm, WqT, b_WqT, bqc, b_bqc),
                    (k_fm, b_kfm, WkT, b_WkT, bkc, b_bkc)):
                for tc in range(4):
                    for dc in range(ND):
                        sch.op("tensor", (lambda e, t_=tc, d_=dc, w=wt:
                                          e.matmul(
                            ps_f[:, :], w[:, d_, :],
                            xn_fm[:, d_, t_ * 512:(t_ + 1) * 512],
                            start=(d_ == 0), stop=(d_ == ND - 1))),
                               reads=[b_wt, b_xnfm], writes=[b_psf], tag="qkm")
                    sch.op("scalar", (lambda e, t_=tc, d2=dst, bb=bias:
                                      e.activation(
                        d2[:, t_ * 512:(t_ + 1) * 512], ps_f[:, :], AF.Identity,
                        bias=bb[:, 0:1])),
                           reads=[b_psf, b_bias], writes=[b_dst], tag="qkev")
            for (dst, b_dst, wt, b_wt, bbc_t, b_bbc_t) in (
                    (v_tm, b_vtm, WvT, b_WvT, bv_bc, b_bvbc),
                    (k_tm, b_ktm, WkT, b_WkT, bk_bc, b_bkbc)):
                for j in range(NT):
                    for dc in range(ND):
                        sch.op("tensor", (lambda e, jj=j, d_=dc, w=wt:
                                          e.matmul(
                            ps_t[:, :], xn_fm[:, d_, jj * 128:(jj + 1) * 128],
                            w[:, d_, :],
                            start=(d_ == 0), stop=(d_ == ND - 1))),
                               reads=[b_wt, b_xnfm], writes=[b_pst], tag="vkm")
                    sch.op("vector", (lambda e, jj=j, d2=dst, bb=bbc_t:
                                      e.tensor_tensor(
                        d2[:, jj, :], ps_t[:, :], bb[:, 0:128], OP.add)),
                           reads=[b_pst, b_bbc_t], writes=[b_dst], tag="vkev")

            S_T, b_ST = sch.sbuf("ST", [128, 128], F32)
            S_Tb, b_STb = sch.sbuf("STb", [128, 128], BF16)
            kscl, b_kscl = sch.sbuf("kscl", [128, 128], BF16)
            intra, b_intra = sch.sbuf("intra", [128, 128], F32)
            P_sb, b_P = sch.sbuf("Psb", [128, 128], BF16)
            ps_a, b_psa = sch.psum("psa", [128, 128], F32)
            ps_i, b_psi = sch.psum("psi", [128, 128], F32)
            ps_n, b_psn = sch.psum("psn", [128, 128], F32)
            ps_s, b_pss = sch.psum("pss", [128, 128], F32)
            sch.op("gpsimd", (lambda e: e.memset(S_T[:, :], 0.0)),
                   writes=[b_ST], tag="st0")
            for g in range(NT):
                sch.op("vector", (lambda e: e.tensor_copy(S_Tb[:, :], S_T[:, :])),
                       reads=[b_ST], writes=[b_STb], tag="stcast")
                sch.op("tensor", (lambda e, gg=g: e.matmul(
                    ps_a[:, :], k_fm[:, gg * 128:(gg + 1) * 128],
                    q_fm[:, gg * 128:(gg + 1) * 128], start=True, stop=True)),
                       reads=[b_kfm, b_qfm], writes=[b_psa], tag="amm")
                sch.op("vector", (lambda e: e.tensor_tensor(
                    P_sb[:, :], ps_a[:, :], dtri[:, :], OP.mult)),
                       reads=[b_psa, b_dtri], writes=[b_P], tag="pmask")
                sch.op("tensor", (lambda e, gg=g: e.matmul(
                    ps_i[:, :], P_sb[:, :], v_tm[:, gg, :],
                    start=True, stop=True)),
                       reads=[b_P, b_vtm], writes=[b_psi], tag="imm")
                sch.op("tensor", (lambda e, gg=g: e.matmul(
                    ps_n[:, :], q_fm[:, gg * 128:(gg + 1) * 128], S_Tb[:, :],
                    start=True, stop=True)),
                       reads=[b_qfm, b_STb], writes=[b_psn], tag="nmm")
                sch.op("scalar", (lambda e: e.copy(intra[:, :], ps_i[:, :])),
                       reads=[b_psi], writes=[b_intra], tag="iev")
                sch.op("vector", (lambda e, gg=g: e.scalar_tensor_tensor(
                    out_tm[:, gg, :], ps_n[:, :], dpow[:, 0:1], intra[:, :],
                    OP.mult, OP.add)),
                       reads=[b_psn, b_dpow, b_intra], writes=[b_otm],
                       tag="oev")
                sch.op("vector", (lambda e, gg=g: e.tensor_scalar(
                    kscl[:, :], k_tm[:, gg, :], dtail[:, 0:1], None, OP.mult)),
                       reads=[b_ktm, b_dtail], writes=[b_kscl], tag="kscl")
                sch.op("tensor", (lambda e, gg=g: e.matmul(
                    ps_s[:, :], kscl[:, :], v_tm[:, gg, :],
                    start=True, stop=True)),
                       reads=[b_kscl, b_vtm], writes=[b_pss], tag="smm")
                sch.op("vector", (lambda e: e.scalar_tensor_tensor(
                    S_T[:, :], S_T[:, :], DEC ** 128, ps_s[:, :],
                    OP.mult, OP.add)),
                       reads=[b_pss, b_ST], writes=[b_ST], tag="supd")

            cc_in = nc.dram_tensor(sch.u("cc3i"), [128, 128], F32).ap()
            cc_out = nc.dram_tensor(sch.u("cc3o"), [256, 128], F32).ap()
            b_c3i, b_c3o = sch.buf("cc3i"), sch.buf("cc3o")
            sch.dma("sync", cc_in, S_T[:, :], reads=[b_ST], writes=[b_c3i])
            sch.cc("AllGather", OP.bypass, PAIRS, [cc_in], [cc_out],
                   reads=[b_c3i], writes=[b_c3o])
            shalf, b_shalf = sch.sbuf("shalf2", [128, 128], F32)
            sch.dma("sync", shalf[:, :], cc_out[0:128, :], reads=[b_c3o],
                    writes=[b_shalf])
            sch.op("vector", (lambda e: e.tensor_scalar(
                shalf[:, :], shalf[:, :], env.hbc[:, 0:1], None, OP.mult)),
                   reads=[b_shalf, env.b_hbc], writes=[b_shalf], tag="hm3")
            sg_b, b_sgb = sch.sbuf("sgb", [128, 128], BF16)
            for g in range(NT):
                sc = float(DEC ** (128 * g))
                sch.op("vector", (lambda e, s=sc: e.tensor_scalar(
                    sg_b[:, :], shalf[:, :], s, None, OP.mult)),
                       reads=[b_shalf], writes=[b_sgb], tag="sgc")
                sch.op("tensor", (lambda e, gg=g: e.matmul(
                    ps_n[:, :], q_fm[:, gg * 128:(gg + 1) * 128], sg_b[:, :],
                    start=True, stop=True)),
                       reads=[b_qfm, b_sgb], writes=[b_psn], tag="cmm")
                sch.op("vector", (lambda e, gg=g: e.scalar_tensor_tensor(
                    out_tm[:, gg, :], ps_n[:, :], dpow[:, 0:1],
                    out_tm[:, gg, :], OP.mult, OP.add)),
                       reads=[b_psn, b_dpow, b_otm], writes=[b_otm],
                       tag="cadd2")

            out_fmT, b_ofmT = sch.sbuf("ofmT", [128, 128], BF16)
            ps_y, b_psy = sch.psum("psy", [128, 512], F32)
            ps_tr3, b_ptr3 = sch.psum("ptr3", [128, 128], BF16)
            ssb, b_ssb = sch.sbuf("ssb2", [128, D], BF16)
            sview = s_dram.rearrange("p (j2 d2) -> p j2 d2", d2=D)
            for j in range(NT):
                sch.op("tensor", (lambda e, jj=j: e.transpose(
                    ps_tr3[:, :], out_tm[:, jj, :], env.ident_bf[:, :])),
                       reads=[b_otm, env.b_ident_bf], writes=[b_ptr3],
                       tag="otr")
                sch.op("vector", (lambda e: e.tensor_copy(out_fmT[:, :],
                                                          ps_tr3[:, :])),
                       reads=[b_ptr3], writes=[b_ofmT], tag="otev")
                for half in range(2):
                    sch.op("tensor", (lambda e, h=half: e.matmul(
                        ps_y[:, :], out_fmT[:, :],
                        WoT[:, 0, h * 512:(h + 1) * 512],
                        start=True, stop=True)),
                           reads=[b_ofmT, b_WoT], writes=[b_psy], tag="ymm")
                    sch.op("vector", (lambda e, h=half: e.tensor_tensor(
                        ssb[:, h * 512:(h + 1) * 512], ps_y[:, :],
                        bo_bc[:, h * 512:(h + 1) * 512], OP.add)),
                           reads=[b_psy, b_bobc], writes=[b_ssb], tag="yev")
                sch.dma("sync", sview[:, j, :], ssb[:, :], reads=[b_ssb],
                        writes=[b_sdram], tag="sst2")

        yacc, b_yacc = sch.sbuf("yacc2", [128, NT, D], BF16)
        wsig, b_wsig = sch.sbuf("wsig", [128, NT, 1], F32)
        with sch.scope():
            mask_t, b_mask = sch.sbuf("mask2", [128, NT, 16], BF16)
            mask_f, b_maskf = sch.sbuf("mask2f", [128, NT, 16], F32)
            gs, _ = sch.sbuf("gs2", [128, 4], F32)
            b_gs = sch.buf("gs2")
            for j in range(NT):
                sch.op("vector", (lambda e, jj=j: e.tensor_reduce(
                    gs[:, 0:1], logits[:, jj, :], AX.X, OP.max)),
                       reads=[b_logits], writes=[b_gs], tag="g2max")
                sch.op("scalar", (lambda e, jj=j: e.activation(
                    wsig[:, jj, :], gs[:, 0:1], AF.Sigmoid)),
                       reads=[b_gs], writes=[b_wsig], tag="g2sig")
                sch.op("vector", (lambda e, jj=j: e.tensor_scalar(
                    mask_f[:, jj, :], logits[:, jj, :], gs[:, 0:1], None,
                    OP.is_ge)),
                       reads=[b_logits, b_gs], writes=[b_maskf], tag="g2m")
                sch.op("vector", (lambda e, jj=j: e.tensor_copy(
                    mask_t[:, jj, :], mask_f[:, jj, :])),
                       reads=[b_maskf], writes=[b_mask], tag="g2mc")
            mask_fmT, b_mfmT = sch.sbuf("mfmT", [16, NT, 128], BF16)
            ps_mt, b_pmt = sch.psum("pmt", [16, 128], BF16)
            for j in range(NT):
                sch.op("tensor", (lambda e, jj=j: e.transpose(
                    ps_mt[:, :], mask_t[:, jj, :], env.ident_bf[:, :])),
                       reads=[b_mask, env.b_ident_bf], writes=[b_pmt],
                       tag="mtr")
                sch.op("vector", (lambda e, jj=j: e.tensor_copy(
                    mask_fmT[:, jj, :], ps_mt[:, :])),
                       reads=[b_pmt], writes=[b_mfmT], tag="mtev")
            b2s, b_b2s = sch.sbuf("b2s", [16, D], BF16)
            b2f, b_b2f = sch.sbuf("b2f", [16, D], F32)
            sch.dma("sync", b2f[:, :], W["L2_ffn_b2"], writes=[b_b2f])
            sch.op("vector", (lambda e: e.tensor_copy(b2s[:, :], b2f[:, :])),
                   reads=[b_b2f], writes=[b_b2s], tag="b2c")
            ps_o2, b_pso2 = sch.psum("pso2b", [128, 2, 512], F32)
            for j in range(NT):
                for half in range(2):
                    sch.op("tensor", (lambda e, jj=j, h=half: e.matmul(
                        ps_o2[:, h, :], mask_fmT[:, jj, :],
                        b2s[:, h * 512:(h + 1) * 512],
                        start=True, stop=True)),
                           reads=[b_mfmT, b_b2s], writes=[b_pso2], tag="b2mm")
                    sch.op("vector", (lambda e, jj=j, h=half: e.tensor_copy(
                        yacc[:, jj, h * 512:(h + 1) * 512], ps_o2[:, h, :])),
                           reads=[b_pso2], writes=[b_yacc], tag="b2ev")

            w1s, b_w1s = sch.sbuf("w1s", [128, 8, 2048], BF16)
            w2s, b_w2s = sch.sbuf("w2s", [128, 16, 1024], BF16)
            wst, b_wst = sch.sbuf("wst", [128, 2, 2048], F32)
            b1c, b_b1c = sch.sbuf("b1c", [128, 16], F32)
            hhg, b_hhg = sch.sbuf("hhg2", [128, 16, 512], BF16)
            ps_h, _ = sch.psum("psh2", [128, 2, 512], F32)
            b_psh2 = [sch.buf("psh2a"), sch.buf("psh2b")]
            for e_i in range(16):
                w1d = W["L2_ffn_w1"][e_i:e_i + 1, :, :].rearrange(
                    "a b c -> (a b) c")
                w2d = W["L2_ffn_w2"][e_i:e_i + 1, :, :].rearrange(
                    "a b c -> (a b) c")
                b1d = W["L2_ffn_b1"][e_i:e_i + 1, :].rearrange("a b -> (a b)")
                sch.dma("sync", b1c[:, :],
                        b1d.rearrange("(c p) -> p c", p=128),
                        writes=[b_b1c], tag="b1l",
                        allow_slow_non_contiguous=True)
                for dc in range(ND):
                    sl = dc % 2
                    sch.dma("sync", wst[:, sl, :],
                            w1d[dc * 128:(dc + 1) * 128, :],
                            writes=[b_wst], tag="w1l")
                    sch.op("vector", (lambda e, d_=dc, s=sl: e.tensor_copy(
                        w1s[:, d_, :], wst[:, s, :])),
                           reads=[b_wst], writes=[b_w1s], tag="w1c")
                for fc in range(16):
                    sl = fc % 2
                    sch.dma("sync", wst[:, sl, 0:1024],
                            w2d[fc * 128:(fc + 1) * 128, :],
                            writes=[b_wst], tag="w2l")
                    sch.op("vector", (lambda e, f_=fc, s=sl: e.tensor_copy(
                        w2s[:, f_, :], wst[:, s, 0:1024])),
                           reads=[b_wst], writes=[b_w2s], tag="w2c")
                for tc in range(4):
                    for fc in range(16):
                        sl = fc % 2
                        for dc in range(ND):
                            sch.op("tensor", (lambda e, f_=fc, t_=tc, d_=dc,
                                              s=sl: e.matmul(
                                ps_h[:, s, :],
                                w1s[:, d_, f_ * 128:(f_ + 1) * 128],
                                xn_fm[:, d_, t_ * 512:(t_ + 1) * 512],
                                start=(d_ == 0), stop=(d_ == ND - 1))),
                                   reads=[b_w1s, b_xnfm], writes=[b_psh2[sl]],
                                   tag="h1mm")
                        sch.op("scalar", (lambda e, f_=fc, s=sl: e.activation(
                            hhg[:, f_, :], ps_h[:, s, :], AF.Gelu,
                            bias=b1c[:, f_:f_ + 1])),
                               reads=[b_psh2[sl], b_b1c], writes=[b_hhg],
                               tag="hgelu")
                    for jr in range(4):
                        j = tc * 4 + jr
                        for half in range(2):
                            for fc in range(16):
                                sch.op("tensor", (lambda e, jj=jr, h=half,
                                                  f_=fc: e.matmul(
                                    ps_o2[:, h, :],
                                    hhg[:, f_, jj * 128:(jj + 1) * 128],
                                    w2s[:, f_, h * 512:(h + 1) * 512],
                                    start=(f_ == 0), stop=(f_ == 15))),
                                       reads=[b_hhg, b_w2s], writes=[b_pso2],
                                       tag="h2mm")
                            sch.op("vector", (lambda e, jj=j, h=half,
                                              ee=e_i:
                                              e.scalar_tensor_tensor(
                                yacc[:, jj, h * 512:(h + 1) * 512],
                                ps_o2[:, h, :],
                                mask_f[:, jj, ee:ee + 1],
                                yacc[:, jj, h * 512:(h + 1) * 512],
                                OP.mult, OP.add)),
                                   reads=[b_pso2, b_maskf, b_yacc],
                                   writes=[b_yacc], tag="hacc")

        emit_combine2(env, xi, xi_buf, xo, xo_buf, s_dram, b_sdram,
                      yacc, b_yacc, wsig, b_wsig)


def emit_combine2(env, xi, xi_buf, xo, xo_buf, s_d, b_sd, yacc, b_yacc,
                  wsig, b_wsig):
    sch = env.sch
    with sch.scope():
        xld, _ = sch.sbuf("xld2", [128, 2, D], F32)
        sld, _ = sch.sbuf("sld2", [128, 2, D], BF16)
        gt, _ = sch.sbuf("gt2", [128, 2, D], BF16)
        mt, _ = sch.sbuf("mt2", [128, 2, D], F32)
        ot, _ = sch.sbuf("ot2", [128, 2, D], F32)
        b_x = [sch.buf("cx0"), sch.buf("cx1")]
        b_s = [sch.buf("cs0"), sch.buf("cs1")]
        b_g = [sch.buf("cg0"), sch.buf("cg1")]
        b_m = [sch.buf("cm0"), sch.buf("cm1")]
        b_o = [sch.buf("co0"), sch.buf("co1")]
        s_view = s_d.rearrange("p (j2 d2) -> p j2 d2", d2=D)
        for j in range(NT):
            sl = j % 2
            sch.dma("sync", xld[:, sl, :], xi[j * 128:(j + 1) * 128, :],
                    reads=[xi_buf], writes=[b_x[sl]], tag="cxl")
            sch.dma("sync", sld[:, sl, :], s_view[:, j, :],
                    reads=[b_sd], writes=[b_s[sl]], tag="csl")
            sch.op("scalar", (lambda e, s=sl, jj=j: e.activation(
                gt[:, s, :], yacc[:, jj, :], AF.Silu,
                scale=wsig[:, jj, 0:1])),
                   reads=[b_yacc, b_wsig], writes=[b_g[sl]], tag="silu2")
            sch.op("vector", (lambda e, s=sl: e.tensor_tensor(
                mt[:, s, :], sld[:, s, :], gt[:, s, :], OP.mult)),
                   reads=[b_s[sl], b_g[sl]], writes=[b_m[sl]], tag="sg2")
            sch.op("vector", (lambda e, s=sl: e.tensor_tensor(
                ot[:, s, :], xld[:, s, :], mt[:, s, :], OP.add)),
                   reads=[b_x[sl], b_m[sl]], writes=[b_o[sl]], tag="xadd2")
            sch.dma("sync", xo[j * 128:(j + 1) * 128, :], ot[:, sl, :],
                    reads=[b_o[sl]], writes=[xo_buf], tag="cst2")


def emit_layer0(env, xi, xi_buf, xo, xo_buf):
    sch = env.sch
    nc = env.nc
    W = env.W
    P = env.P
    SLAB = 1024
    NSLAB = 4
    TWO_PI = 6.283185307179586
    HALF_PI = 1.5707963267948966
    with sch.scope():
        s_dram = nc.dram_tensor(sch.u("s0_d"), [128, NT * D], BF16).ap()
        b_sdram = sch.buf("s0_d")
        hbar, b_hbar = sch.sbuf("hbar0", [128, 1], F32)
        sch.op("vector", (lambda e: e.tensor_scalar(
            hbar[:, :], env.hbc[:, :], -1.0, 1.0, OP.mult, OP.add)),
               reads=[env.b_hbc], writes=[b_hbar], tag="hbar0")
        logits, b_logits = sch.sbuf("lg0", [128, NT, 8], F32)
        rw_t, b_rwt = sch.sbuf("rw0", [128, ND, 8], F32)
        for c in range(ND):
            sch.dma("sync", rw_t[:, c, :],
                    W["L0_ffn_router_w"][:, c * 128:(c + 1) * 128]
                    .rearrange("e p -> p e"),
                    writes=[b_rwt], tag="rwl0",
                    allow_slow_non_contiguous=True)
        with sch.scope():
            outs = emit_ln_phase(env, xi, xi_buf,
                                 {"xn_fm": "dram", "logits": "sbuf"},
                                 router=(rw_t, b_rwt, 8),
                                 prealloc={"logits": (logits, b_logits)})
            d_xn, b_dxn = outs["xn_fm"]
        rb_bc, b_rbbc = bcast_load(env, W["L0_ffn_router_b"], 8, "rb0")
        sch.op("vector", (lambda e: e.tensor_tensor(
            logits[:, :, :], logits[:, :, :],
            rb_bc[:, 0:8].unsqueeze(1).broadcast_to([128, NT, 8]), OP.add)),
               reads=[b_logits, b_rbbc], writes=[b_logits], tag="lgb0")

        ccx_out = nc.dram_tensor(sch.u("ccxo"), [256, ND * T], BF16).ap()
        b_cxo = sch.buf("ccxo")
        sch.cc("AllGather", OP.bypass, PAIRS, [d_xn], [ccx_out],
               reads=[b_dxn], writes=[b_cxo])
        xng = ccx_out.rearrange("(s p) (c t) -> p s c t", p=128, t=T)

        ccs_in = nc.dram_tensor(sch.u("ccsi"), [128, 32 * D], F32).ap()
        b_csi = sch.buf("ccsi")
        ccs_out = nc.dram_tensor(sch.u("ccso"), [256, 32 * D], F32).ap()
        b_cso = sch.buf("ccso")
        sview_in = ccs_in.rearrange("p (tt d2) -> p tt d2", d2=D)

        with sch.scope():
            selq, b_selq = sch.sbuf("selq", [128, 8, 2, 128], BF16)
            sch.dma("sync", selq[:, :, :, :], env.consts_dram["sel_q"],
                    writes=[b_selq], tag="sql")
            eloc, b_eloc = sch.sbuf("eloc", [16, 4, 128], F32)
            sch.dma("sync", eloc[:, :, :], env.consts_dram["e_loc"],
                    writes=[b_eloc], tag="ell")
            erep, b_erep = sch.sbuf("erep", [32, 128], F32)
            sch.dma("sync", erep[:, :], env.consts_dram["e_rep"],
                    writes=[b_erep], tag="erl")
            wk, b_wk = sch.sbuf("wk0", [128, 8, 512], BF16)
            wqre, b_wqre = sch.sbuf("wqre", [128, 8, 512], BF16)
            wqim, b_wqim = sch.sbuf("wqim", [128, 8, 512], BF16)
            ws, b_ws = sch.sbuf("ws0", [128, 8, 16], BF16)
            wout_sb, b_wout = sch.sbuf("wo0", [128, 4, 1024], BF16)
            with sch.scope():
                stg, _ = sch.sbuf("wstg0", [128, 2, 1024], F32)
                b_stg = [sch.buf("wsg0"), sch.buf("wsg1")]
                loads = []
                for cin in range(ND):
                    loads.append((P["l0w_k"][cin * 128:(cin + 1) * 128, :],
                                  wk, b_wk, cin, 512))
                    loads.append((P["l0w_qre"][cin * 128:(cin + 1) * 128, :],
                                  wqre, b_wqre, cin, 512))
                    loads.append((P["l0w_qim"][cin * 128:(cin + 1) * 128, :],
                                  wqim, b_wqim, cin, 512))
                    loads.append((P["l0w_s"][cin * 128:(cin + 1) * 128, :],
                                  ws, b_ws, cin, 16))
                for ci in range(4):
                    loads.append((P["l0wout"][ci * 128:(ci + 1) * 128, :],
                                  wout_sb, b_wout, ci, 1024))
                for i, (src, dstt, b_dst, c, wdt) in enumerate(loads):
                    sl = i % 2
                    sch.dma("sync", stg[:, sl, 0:wdt], src,
                            writes=[b_stg[sl]], tag="wl0")
                    sch.op("vector", (lambda e, s=sl, t_=dstt, cc=c, w_=wdt:
                                      e.tensor_copy(t_[:, cc, :],
                                                    stg[:, s, 0:w_])),
                           reads=[b_stg[sl]], writes=[b_dst], tag="wc0")
            th, b_th = sch.sbuf("th0", [128, 4, 8], F32)
            sch.dma("sync", th[:, :, :],
                    P["l0theta"].rearrange("(c p) m -> p c m", p=128),
                    writes=[b_th], tag="thl", allow_slow_non_contiguous=True)
            sch.op("vector", (lambda e: e.tensor_scalar(
                th[:, :, :], th[:, :, :], 1.0 / TWO_PI, None, OP.mult)),
                   reads=[b_th], writes=[b_th], tag="thsc")
            ckk, b_ckk = sch.sbuf("ckk0", [128, 4, 4], F32)
            sch.dma("sync", ckk[:, :, :],
                    P["l0ck_k"].rearrange("(c p) j -> p c j", p=128),
                    writes=[b_ckk], tag="ckl", allow_slow_non_contiguous=True)
            cks, b_cks = sch.sbuf("cks0", [16, 4], F32)
            sch.dma("sync", cks[:, :], P["l0ck_s"], writes=[b_cks], tag="csl")
            slp, b_slp = sch.sbuf("slp0", [16, 4], F32)
            sch.dma("sync", slp[:, 0:1],
                    P["l0slope"].rearrange("(p one) -> p one", one=1),
                    writes=[b_slp], tag="sll", allow_slow_non_contiguous=True)
            sch.op("scalar", (lambda e: e.activation(
                slp[:, 1:2], slp[:, 0:1], AF.Exp)),
                   reads=[b_slp], writes=[b_slp], tag="spe")
            sch.op("scalar", (lambda e: e.activation(
                slp[:, 2:3], slp[:, 1:2], AF.Ln, bias=1.0)),
                   reads=[b_slp], writes=[b_slp], tag="spl")
            sch.op("vector", (lambda e: e.tensor_scalar(
                slp[:, 3:4], slp[:, 2:3], -1.0, None, OP.mult)),
                   reads=[b_slp], writes=[b_slp], tag="spn")
            ksel, b_ksel = sch.sbuf("ksel0", [16, 2], F32)
            sch.dma("sync", ksel[:, 0:1],
                    P["l0kksel"].rearrange("(p one) -> p one", one=1),
                    writes=[b_ksel], tag="ksl", allow_slow_non_contiguous=True)
            sch.op("vector", (lambda e: e.tensor_scalar(
                ksel[:, 1:2], ksel[:, 0:1], -1.0, 1.0, OP.mult, OP.add)),
                   reads=[b_ksel], writes=[b_ksel], tag="ksb")
            sscl, b_sscl = sch.sbuf("sscl0", [16, 1], F32)
            sch.dma("sync", sscl[:, 0:1],
                    P["l0sscale"].rearrange("(p one) -> p one", one=1),
                    writes=[b_sscl], tag="ssl", allow_slow_non_contiguous=True)
            screp, b_screp = sch.sbuf("screp", [128, 2], F32)
            with sch.scope():
                wres, b_wres = sch.sbuf("wres", [32, 32], F32)
                wims, b_wims = sch.sbuf("wims", [32, 32], F32)
                nsc, b_nsc = sch.sbuf("nsc0", [32, 2], F32)
                scp, b_scp = sch.sbuf("scp0", [32, 2], F32)
                dcl, b_dcl = sch.sbuf("dcl0", [32, 2], F32)
                ps_sc, b_pssc = sch.psum("pssc", [128, 2], F32)
                sch.dma("sync", wres[:, :], W["L0_attn_W_re"],
                        writes=[b_wres], tag="wrel")
                sch.dma("sync", wims[:, :], W["L0_attn_W_im"],
                        writes=[b_wims], tag="wiml")
                nsv = W["L0_attn_norm_scale"].rearrange(
                    "(two p) -> two p", p=32)
                sch.dma("sync", nsc[:, 0:1],
                        nsv[0:1, :].rearrange("one p -> p one"),
                        writes=[b_nsc], tag="nsl",
                        allow_slow_non_contiguous=True)
                sch.dma("sync", nsc[:, 1:2],
                        nsv[1:2, :].rearrange("one p -> p one"),
                        writes=[b_nsc], tag="nsl2",
                        allow_slow_non_contiguous=True)
                for (wt_, b_wt_, col) in ((wres, b_wres, 0), (wims, b_wims, 1)):
                    sch.op("vector", (lambda e, t_=wt_: e.tensor_tensor(
                        t_[:, :], t_[:, :], env.ident_f32[0:32, 0:32],
                        OP.mult)),
                           reads=[b_wt_, env.b_ident_f32], writes=[b_wt_],
                           tag="diag")
                    sch.op("vector", (lambda e, t_=wt_, cc=col:
                                      e.tensor_reduce(
                        dcl[:, cc:cc + 1], t_[:, :], AX.X, OP.add)),
                           reads=[b_wt_], writes=[b_dcl], tag="dred")
                sch.op("vector", (lambda e: e.tensor_tensor(
                    scp[:, :], dcl[:, :], nsc[:, :], OP.mult)),
                       reads=[b_dcl, b_nsc], writes=[b_scp], tag="scre")
                sch.op("tensor", (lambda e: e.matmul(
                    ps_sc[:, :], erep[:, :], scp[:, :],
                    start=True, stop=True)),
                       reads=[b_erep, b_scp], writes=[b_pssc], tag="scmm")
                sch.op("vector", (lambda e: e.tensor_copy(
                    screp[:, :], ps_sc[:, :])),
                       reads=[b_pssc], writes=[b_screp], tag="scev")
            ones_t, b_ones = sch.sbuf("ones0", [128, SLAB], F32)
            sch.op("gpsimd", (lambda e: e.memset(ones_t[:, :], 1.0)),
                   writes=[b_ones], tag="one0")
            hpic, b_hpic = sch.sbuf("hpic0", [128, 1], F32)
            sch.op("gpsimd", (lambda e: e.memset(hpic[:, :], HALF_PI)),
                   writes=[b_hpic], tag="hpi0")
            carr, b_carr = sch.sbuf("carr0", [128, 4, 8, 2], F32)
            sch.op("gpsimd", (lambda e: e.memset(carr[:, :, :, :], 0.0)),
                   writes=[b_carr], tag="car0")
            denc, b_denc = sch.sbuf("denc0", [16, 1], F32)
            sch.op("gpsimd", (lambda e: e.memset(denc[:, :], 0.0)),
                   writes=[b_denc], tag="dnc0")
            ztk, b_ztk = sch.sbuf("ztk0", [128, 4, 3], F32)
            sch.op("gpsimd", (lambda e: e.memset(ztk[:, :, :], 0.0)),
                   writes=[b_ztk], tag="ztk0")
            zts, b_zts = sch.sbuf("zts0", [16, 3], F32)
            sch.op("gpsimd", (lambda e: e.memset(zts[:, :], 0.0)),
                   writes=[b_zts], tag="zts0")
            ps_q, b_psq = sch.psum("psq0", [128, 512], F32)
            ps_z, b_psz = sch.psum("psz0", [128, 512], F32)
            ps_s16, b_pss16 = sch.psum("pss16", [16, 512], F32)
            ps_pe, b_pspe = sch.psum("pspe0", [128, 512], F32)
            ps_qe, b_psqe = sch.psum("psqe0", [128, 512], F32)
            ps_w, b_psw = sch.psum("psw0", [128, 512], F32)

            xnsl, b_xnsl = sch.sbuf("xnsl", [128, ND, SLAB], BF16)
            qsl, b_qsl = sch.sbuf("qsl0", [128, 2, 4, SLAB], BF16)
            pwv, b_pwv = sch.sbuf("pwv0", [16, SLAB], F32)
            inv_t, b_invt = sch.sbuf("invt0", [16, SLAB], F32)
            zps, b_zps = sch.sbuf("zps0", [16, SLAB + 3], F32)
            sconv, b_sconv = sch.sbuf("scv0", [16, SLAB], F32)
            prow, b_prow = sch.sbuf("prw0", [16, SLAB], F32)
            drow, b_drow = sch.sbuf("drw0", [16, SLAB], F32)
            den_t, b_dent = sch.sbuf("den0", [16, SLAB], F32)
            ysl, b_ysl = sch.sbuf("ysl0", [128, 4, SLAB], BF16)
            qxr, b_qxr = sch.sbuf("qxr0", [128, 8, SLAB], BF16)
            qxi, b_qxi = sch.sbuf("qxi0", [128, 8, SLAB], BF16)
            zpk, b_zpk = sch.sbuf("zpk0", [128, SLAB + 3], F32)
            kv, b_kv = sch.sbuf("kv0", [128, SLAB], F32)
            pwe, b_pwe = sch.sbuf("pwe0", [128, SLAB], F32)
            ive, b_ive = sch.sbuf("ive0", [128, SLAB], F32)
            acr, b_acr = sch.sbuf("acr0", [128, SLAB], F32)
            aci, b_aci = sch.sbuf("aci0", [128, SLAB], F32)
            yt, b_yt = sch.sbuf("yt0", [128, SLAB], F32)
            ut, b_ut = sch.sbuf("ut0", [128, SLAB], F32)
            kf, b_kf = sch.sbuf("kf0", [128, SLAB], F32)
            rek, b_rek = sch.sbuf("rek0", [128, SLAB], F32)
            imk, b_imk = sch.sbuf("imk0", [128, SLAB], F32)
            kint, b_kint = sch.sbuf("kint0", [128, SLAB], I32)
            cumr, b_cumr = sch.sbuf("cumr0", [128, SLAB], F32)
            cumi, b_cumi = sch.sbuf("cumi0", [128, SLAB], F32)
            sst, b_sst = sch.sbuf("sst0", [128, 512], F32)

            for slab in range(NSLAB):
                t0 = slab * SLAB
                hs = slab // 2
                off = (slab % 2) * SLAB
                sch.dma("sync", xnsl[:, :, :], xng[:, hs, :, off:off + SLAB],
                        reads=[b_cxo], writes=[b_xnsl], tag="xnsl")
                for reim, (wq, b_wq) in enumerate(((wqre, b_wqre),
                                                   (wqim, b_wqim))):
                    for qc in range(4):
                        for sub in range(2):
                            for cin in range(ND):
                                sch.op("tensor", (lambda e, w=wq, c_=cin,
                                                  q=qc, s=sub: e.matmul(
                                    ps_q[:, :], w[:, c_, q * 128:(q + 1) * 128],
                                    xnsl[:, c_, s * 512:(s + 1) * 512],
                                    start=(c_ == 0), stop=(c_ == ND - 1))),
                                       reads=[b_wq, b_xnsl], writes=[b_psq],
                                       tag="qmm0")
                            sch.op("vector", (lambda e, r=reim, q=qc, s=sub:
                                              e.tensor_copy(
                                qsl[:, r, q, s * 512:(s + 1) * 512],
                                ps_q[:, :])),
                                   reads=[b_psq], writes=[b_qsl], tag="qev0")
                sch.op("vector", (lambda e: e.tensor_copy(
                    zps[:, 0:3], zts[:, :])),
                       reads=[b_zts], writes=[b_zps], tag="zsc0")
                for sub in range(2):
                    for cin in range(ND):
                        sch.op("tensor", (lambda e, c_=cin, s=sub:
                                          e.matmul(
                            ps_s16[:, :], ws[:, c_, :],
                            xnsl[:, c_, s * 512:(s + 1) * 512],
                            start=(c_ == 0), stop=(c_ == ND - 1))),
                               reads=[b_ws, b_xnsl], writes=[b_pss16],
                               tag="smm0")
                    sch.op("vector", (lambda e, s=sub: e.tensor_copy(
                        zps[:, 3 + s * 512:3 + (s + 1) * 512],
                        ps_s16[:, :])),
                           reads=[b_pss16], writes=[b_zps], tag="sev0")
                sch.op("vector", (lambda e: e.tensor_copy(
                    zts[:, :], zps[:, SLAB:SLAB + 3])),
                       reads=[b_zps], writes=[b_zts], tag="zsu0")
                sch.op("vector", (lambda e: e.tensor_scalar(
                    sconv[:, :], zps[:, 0:SLAB], cks[:, 0:1], None,
                    OP.mult)),
                       reads=[b_zps, b_cks], writes=[b_sconv], tag="scv1")
                for j in range(1, 4):
                    sch.op("vector", (lambda e, jj=j:
                                      e.scalar_tensor_tensor(
                        sconv[:, :], zps[:, jj:jj + SLAB],
                        cks[:, jj:jj + 1], sconv[:, :],
                        OP.mult, OP.add)),
                           reads=[b_zps, b_cks, b_sconv],
                           writes=[b_sconv], tag="scv2")
                sch.op("vector", (lambda e: e.tensor_scalar(
                    sconv[:, :], sconv[:, :], sscl[:, 0:1], None,
                    OP.mult)),
                       reads=[b_sconv, b_sscl], writes=[b_sconv],
                       tag="lgp0")
                sch.op("vector", (lambda e: e.tensor_scalar(
                    sconv[:, :], sconv[:, :], -20.0, 20.0,
                    OP.max, OP.min)),
                       reads=[b_sconv], writes=[b_sconv], tag="clp0")
                sch.op("gpsimd", (lambda e, b_=t0: e.iota(
                    prow[:, :], [[1, SLAB]], base=b_, channel_multiplier=0,
                    allow_small_or_imprecise_dtypes=True)),
                       writes=[b_prow], tag="pio0")
                sch.op("vector", (lambda e: e.tensor_scalar(
                    drow[:, :], prow[:, :], -1.0, 4095.0,
                    OP.mult, OP.add)),
                       reads=[b_prow], writes=[b_drow], tag="dst0")
                sch.op("vector", (lambda e: e.tensor_scalar(
                    prow[:, :], prow[:, :], ksel[:, 0:1], None, OP.mult)),
                       reads=[b_prow, b_ksel], writes=[b_prow],
                       tag="psl0")
                sch.op("vector", (lambda e: e.scalar_tensor_tensor(
                    prow[:, :], drow[:, :], ksel[:, 1:2], prow[:, :],
                    OP.mult, OP.add)),
                       reads=[b_drow, b_ksel, b_prow], writes=[b_prow],
                       tag="dsl0")
                sch.op("vector", (lambda e: e.tensor_scalar(
                    prow[:, :], prow[:, :], slp[:, 3:4], None, OP.mult)),
                       reads=[b_prow, b_slp], writes=[b_prow], tag="ltw0")
                sch.op("vector", (lambda e: e.tensor_tensor(
                    sconv[:, :], sconv[:, :], prow[:, :], OP.add)),
                       reads=[b_prow, b_sconv], writes=[b_sconv],
                       tag="lpw0")
                sch.op("scalar", (lambda e: e.activation(
                    pwv[:, :], sconv[:, :], AF.Exp)),
                       reads=[b_sconv], writes=[b_pwv], tag="pwe0")
                sch.op("vector", (lambda e: e.tensor_tensor_scan(
                    den_t[:, :], ones_t[0:16, :], pwv[:, :], 0.0,
                    OP.mult, OP.add)),
                       reads=[b_ones, b_pwv], writes=[b_dent], tag="dsc0")
                sch.op("vector", (lambda e: e.tensor_tensor(
                    den_t[:, :], den_t[:, :],
                    denc[:, 0:1].broadcast_to([16, SLAB]), OP.add)),
                       reads=[b_dent, b_denc], writes=[b_dent],
                       tag="dca0")
                sch.op("vector", (lambda e: e.tensor_copy(
                    denc[:, 0:1], den_t[:, SLAB - 1:SLAB])),
                       reads=[b_dent], writes=[b_denc], tag="dcu0")
                sch.op("vector", (lambda e: e.tensor_scalar(
                    den_t[:, :], den_t[:, :], 1.0e-4, None, OP.max)),
                       reads=[b_dent], writes=[b_dent], tag="dmx0")
                sch.op("vector", (lambda e: e.reciprocal(
                    inv_t[:, :], den_t[:, :])),
                       reads=[b_dent], writes=[b_invt], tag="din0")
                for kqg in range(2):
                    for m in range(8):
                        for (dstq, b_dq, reim) in ((qxr, b_qxr, 0),
                                                   (qxi, b_qxi, 1)):
                            for sub in range(2):
                                sch.op("tensor", (lambda e, mm=m, r=reim,
                                                  s=sub, k=kqg: e.matmul(
                                    ps_qe[:, :], selq[:, mm, 0, :],
                                    qsl[:, r, k * 2, s * 512:(s + 1) * 512],
                                    start=True, stop=False)),
                                       reads=[b_selq, b_qsl], writes=[b_psqe],
                                       tag="sqm0")
                                sch.op("tensor", (lambda e, mm=m, r=reim,
                                                  s=sub, k=kqg: e.matmul(
                                    ps_qe[:, :], selq[:, mm, 1, :],
                                    qsl[:, r, k * 2 + 1,
                                        s * 512:(s + 1) * 512],
                                    start=False, stop=True)),
                                       reads=[b_selq, b_qsl], writes=[b_psqe],
                                       tag="sqm1")
                                sch.op("vector", (lambda e, mm=m, t_=dstq,
                                                  s=sub: e.tensor_copy(
                                    t_[:, mm, s * 512:(s + 1) * 512],
                                    ps_qe[:, :])),
                                       reads=[b_psqe], writes=[b_dq],
                                       tag="sqe0")
                    for cc in range(2):
                        ci = kqg * 2 + cc
                        sch.op("vector", (lambda e, c_=ci: e.tensor_copy(
                            zpk[:, 0:3], ztk[:, c_, :])),
                               reads=[b_ztk], writes=[b_zpk], tag="zkc0")
                        for sub in range(2):
                            for cin in range(ND):
                                sch.op("tensor", (lambda e, c_=cin, s=sub,
                                                  o=ci: e.matmul(
                                    ps_z[:, :],
                                    wk[:, c_, o * 128:(o + 1) * 128],
                                    xnsl[:, c_, s * 512:(s + 1) * 512],
                                    start=(c_ == 0), stop=(c_ == ND - 1))),
                                       reads=[b_wk, b_xnsl], writes=[b_psz],
                                       tag="zmm0")
                            sch.op("vector", (lambda e, s=sub: e.tensor_copy(
                                zpk[:, 3 + s * 512:3 + (s + 1) * 512],
                                ps_z[:, :])),
                                   reads=[b_psz], writes=[b_zpk], tag="zev0")
                        sch.op("vector", (lambda e, c_=ci: e.tensor_copy(
                            ztk[:, c_, :], zpk[:, SLAB:SLAB + 3])),
                               reads=[b_zpk], writes=[b_ztk], tag="zku0")
                        sch.op("vector", (lambda e, c_=ci: e.tensor_scalar(
                            kv[:, :], zpk[:, 0:SLAB], ckk[:, c_, 0:1], None,
                            OP.mult)),
                               reads=[b_zpk, b_ckk], writes=[b_kv],
                               tag="cnv0")
                        for j in range(1, 4):
                            sch.op("vector", (lambda e, jj=j, c_=ci:
                                              e.scalar_tensor_tensor(
                                kv[:, :], zpk[:, jj:jj + SLAB],
                                ckk[:, c_, jj:jj + 1], kv[:, :],
                                OP.mult, OP.add)),
                                   reads=[b_zpk, b_ckk, b_kv], writes=[b_kv],
                                   tag="cnv1")
                        for (srct, b_src, dstt, b_dst2) in (
                                (pwv, b_pwv, pwe, b_pwe),
                                (inv_t, b_invt, ive, b_ive)):
                            for sub in range(2):
                                sch.op("tensor", (lambda e, st=srct, s=sub,
                                                  c_=ci: e.matmul(
                                    ps_pe[:, :], eloc[:, c_, :],
                                    st[:, s * 512:(s + 1) * 512],
                                    start=True, stop=True)),
                                       reads=[b_eloc, b_src], writes=[b_pspe],
                                       tag="pem0")
                                sch.op("vector", (lambda e, t_=dstt, s=sub:
                                                  e.tensor_copy(
                                    t_[:, s * 512:(s + 1) * 512],
                                    ps_pe[:, :])),
                                       reads=[b_pspe], writes=[b_dst2],
                                       tag="pee0")
                        for m in range(8):
                            sch.op("vector", (lambda e, c_=ci, mm=m:
                                              e.tensor_scalar(
                                yt[:, :], kv[:, :], th[:, c_, mm:mm + 1],
                                None, OP.mult)),
                                   reads=[b_kv, b_th], writes=[b_yt],
                                   tag="phi0")
                            sch.op("vector", (lambda e: e.tensor_copy(
                                kint[:, :], yt[:, :])),
                                   reads=[b_yt], writes=[b_kint], tag="ki0")
                            sch.op("vector", (lambda e: e.tensor_copy(
                                kf[:, :], kint[:, :])),
                                   reads=[b_kint], writes=[b_kf], tag="kf0")
                            sch.op("vector", (lambda e: e.tensor_tensor(
                                ut[:, :], yt[:, :], kf[:, :], OP.subtract)),
                                   reads=[b_yt, b_kf], writes=[b_ut],
                                   tag="u0")
                            sch.op("vector", (lambda e: e.tensor_scalar(
                                kf[:, :], ut[:, :], 0.5, None, OP.is_ge)),
                                   reads=[b_ut], writes=[b_kf], tag="uw0")
                            sch.op("vector", (lambda e: e.tensor_tensor(
                                ut[:, :], ut[:, :], kf[:, :], OP.subtract)),
                                   reads=[b_ut, b_kf], writes=[b_ut],
                                   tag="uw1")
                            sch.op("vector", (lambda e: e.tensor_scalar(
                                kf[:, :], ut[:, :], -0.5, None, OP.is_lt)),
                                   reads=[b_ut], writes=[b_kf], tag="uw2")
                            sch.op("vector", (lambda e: e.tensor_tensor(
                                ut[:, :], ut[:, :], kf[:, :], OP.add)),
                                   reads=[b_ut, b_kf], writes=[b_ut],
                                   tag="uw3")
                            sch.op("scalar", (lambda e: e.activation(
                                imk[:, :], ut[:, :], AF.Sin, scale=TWO_PI)),
                                   reads=[b_ut], writes=[b_imk], tag="sin0")
                            sch.op("vector", (lambda e: e.tensor_scalar(
                                kf[:, :], ut[:, :], 0.25, None, OP.is_ge)),
                                   reads=[b_ut], writes=[b_kf], tag="cm0")
                            sch.op("vector", (lambda e: e.tensor_tensor(
                                yt[:, :], ut[:, :], kf[:, :], OP.subtract)),
                                   reads=[b_ut, b_kf], writes=[b_yt],
                                   tag="cw0")
                            sch.op("scalar", (lambda e: e.activation(
                                rek[:, :], yt[:, :], AF.Sin, scale=TWO_PI,
                                bias=hpic[:, 0:1])),
                                   reads=[b_yt, b_hpic], writes=[b_rek],
                                   tag="cos0")
                            sch.op("vector", (lambda e: e.tensor_tensor(
                                rek[:, :], rek[:, :], pwe[:, :], OP.mult)),
                                   reads=[b_rek, b_pwe], writes=[b_rek],
                                   tag="tr0")
                            sch.op("vector", (lambda e: e.tensor_tensor(
                                imk[:, :], imk[:, :], pwe[:, :], OP.mult)),
                                   reads=[b_imk, b_pwe], writes=[b_imk],
                                   tag="ti0")
                            sch.op("vector", (lambda e, c_=ci, mm=m:
                                              e.tensor_tensor(
                                rek[:, 0:1], rek[:, 0:1],
                                carr[:, c_, mm, 0:1], OP.add)),
                                   reads=[b_rek, b_carr], writes=[b_rek],
                                   tag="ccr0")
                            sch.op("vector", (lambda e, c_=ci, mm=m:
                                              e.tensor_tensor(
                                imk[:, 0:1], imk[:, 0:1],
                                carr[:, c_, mm, 1:2], OP.add)),
                                   reads=[b_imk, b_carr], writes=[b_imk],
                                   tag="cci0")
                            sch.op("vector", (lambda e: e.tensor_tensor_scan(
                                cumr[:, :], ones_t[:, :], rek[:, :], 0.0,
                                OP.mult, OP.add)),
                                   reads=[b_ones, b_rek], writes=[b_cumr],
                                   tag="scr0")
                            sch.op("vector", (lambda e: e.tensor_tensor_scan(
                                cumi[:, :], ones_t[:, :], imk[:, :], 0.0,
                                OP.mult, OP.add)),
                                   reads=[b_ones, b_imk], writes=[b_cumi],
                                   tag="sci0")
                            sch.op("vector", (lambda e, c_=ci, mm=m:
                                              e.tensor_copy(
                                carr[:, c_, mm, 0:1],
                                cumr[:, SLAB - 1:SLAB])),
                                   reads=[b_cumr], writes=[b_carr],
                                   tag="cur0")
                            sch.op("vector", (lambda e, c_=ci, mm=m:
                                              e.tensor_copy(
                                carr[:, c_, mm, 1:2],
                                cumi[:, SLAB - 1:SLAB])),
                                   reads=[b_cumi], writes=[b_carr],
                                   tag="cui0")
                            if m == 0:
                                sch.op("vector", (lambda e, mm=m:
                                                  e.tensor_tensor(
                                    acr[:, :], cumr[:, :], qxr[:, mm, :],
                                    OP.mult)),
                                       reads=[b_cumr, b_qxr], writes=[b_acr],
                                       tag="pa0")
                                sch.op("vector", (lambda e, mm=m:
                                                  e.tensor_tensor(
                                    aci[:, :], cumi[:, :], qxr[:, mm, :],
                                    OP.mult)),
                                       reads=[b_cumi, b_qxr], writes=[b_aci],
                                       tag="pb0")
                            else:
                                sch.op("vector", (lambda e, mm=m:
                                                  e.tensor_tensor(
                                    rek[:, :], cumr[:, :], qxr[:, mm, :],
                                    OP.mult)),
                                       reads=[b_cumr, b_qxr], writes=[b_rek],
                                       tag="pa1")
                                sch.op("vector", (lambda e: e.tensor_tensor(
                                    acr[:, :], acr[:, :], rek[:, :], OP.add)),
                                       reads=[b_rek, b_acr], writes=[b_acr],
                                       tag="pa2")
                                sch.op("vector", (lambda e, mm=m:
                                                  e.tensor_tensor(
                                    rek[:, :], cumi[:, :], qxr[:, mm, :],
                                    OP.mult)),
                                       reads=[b_cumi, b_qxr], writes=[b_rek],
                                       tag="pb1")
                                sch.op("vector", (lambda e: e.tensor_tensor(
                                    aci[:, :], aci[:, :], rek[:, :], OP.add)),
                                       reads=[b_rek, b_aci], writes=[b_aci],
                                       tag="pb2")
                            sch.op("vector", (lambda e, mm=m:
                                              e.tensor_tensor(
                                rek[:, :], cumi[:, :], qxi[:, mm, :],
                                OP.mult)),
                                   reads=[b_cumi, b_qxi], writes=[b_rek],
                                   tag="pc1")
                            sch.op("vector", (lambda e: e.tensor_tensor(
                                acr[:, :], acr[:, :], rek[:, :], OP.add)),
                                   reads=[b_rek, b_acr], writes=[b_acr],
                                   tag="pc2")
                            sch.op("vector", (lambda e, mm=m:
                                              e.tensor_tensor(
                                rek[:, :], cumr[:, :], qxi[:, mm, :],
                                OP.mult)),
                                   reads=[b_cumr, b_qxi], writes=[b_rek],
                                   tag="pd1")
                            sch.op("vector", (lambda e: e.tensor_tensor(
                                aci[:, :], aci[:, :], rek[:, :],
                                OP.subtract)),
                                   reads=[b_rek, b_aci], writes=[b_aci],
                                   tag="pd2")
                        sch.op("vector", (lambda e: e.tensor_tensor(
                            acr[:, :], acr[:, :], ive[:, :], OP.mult)),
                               reads=[b_acr, b_ive], writes=[b_acr],
                               tag="fin0")
                        sch.op("vector", (lambda e: e.tensor_tensor(
                            aci[:, :], aci[:, :], ive[:, :], OP.mult)),
                               reads=[b_aci, b_ive], writes=[b_aci],
                               tag="fin1")
                        sch.op("vector", (lambda e: e.tensor_scalar(
                            acr[:, :], acr[:, :], screp[:, 0:1], None,
                            OP.mult)),
                               reads=[b_acr, b_screp], writes=[b_acr],
                               tag="fin2")
                        sch.op("vector", (lambda e: e.tensor_scalar(
                            aci[:, :], aci[:, :], screp[:, 1:2], None,
                            OP.mult)),
                               reads=[b_aci, b_screp], writes=[b_aci],
                               tag="fin3")
                        sch.op("vector", (lambda e: e.tensor_tensor(
                            acr[:, :], acr[:, :], aci[:, :], OP.add)),
                               reads=[b_acr, b_aci], writes=[b_acr],
                               tag="fin4")
                        sch.op("scalar", (lambda e, c_=ci: e.activation(
                            ysl[:, c_, :], acr[:, :], AF.Silu)),
                               reads=[b_acr], writes=[b_ysl], tag="fin5")
                for tt in range(8):
                    for dh in range(2):
                        for ci in range(4):
                            sch.op("tensor", (lambda e, t_=tt, d_=dh,
                                              c_=ci: e.matmul(
                                ps_w[:, :],
                                ysl[:, c_, t_ * 128:(t_ + 1) * 128],
                                wout_sb[:, c_, d_ * 512:(d_ + 1) * 512],
                                start=(c_ == 0), stop=(c_ == 3))),
                                   reads=[b_ysl, b_wout], writes=[b_psw],
                                   tag="wom0")
                        sch.op("vector", (lambda e: e.tensor_copy(
                            sst[:, :], ps_w[:, :])),
                               reads=[b_psw], writes=[b_sst], tag="woe0")
                        sch.dma("sync",
                                sview_in[:, slab * 8 + tt,
                                         dh * 512:(dh + 1) * 512],
                                sst[:, :], reads=[b_sst], writes=[b_csi],
                                tag="wos0")
        sch.cc("AllGather", OP.bypass, PAIRS, [ccs_in], [ccs_out],
               reads=[b_csi], writes=[b_cso])
        with sch.scope():
            sg = ccs_out.rearrange("(s p) (tt d2) -> p s tt d2", p=128, d2=D)
            cA, b_cA = sch.sbuf("scA", [128, D], F32)
            cB, b_cB = sch.sbuf("scB", [128, D], F32)
            cC, b_cC = sch.sbuf("scC", [128, D], F32)
            cD, b_cD = sch.sbuf("scD", [128, D], F32)
            st, b_st = sch.sbuf("sts", [128, D], F32)
            stb, b_stb = sch.sbuf("stb", [128, D], BF16)
            sv_out = s_dram.rearrange("p (j2 d2) -> p j2 d2", d2=D)
            for j in range(NT):
                sch.dma("sync", cA[:, :], sg[:, 0, j, :], reads=[b_cso],
                        writes=[b_cA], tag="sla")
                sch.dma("sync", cB[:, :], sg[:, 1, j, :], reads=[b_cso],
                        writes=[b_cB], tag="slb")
                sch.dma("sync", cC[:, :], sg[:, 0, 16 + j, :], reads=[b_cso],
                        writes=[b_cC], tag="slc")
                sch.dma("sync", cD[:, :], sg[:, 1, 16 + j, :], reads=[b_cso],
                        writes=[b_cD], tag="sld")
                sch.op("vector", (lambda e: e.tensor_tensor(
                    st[:, :], cA[:, :], cB[:, :], OP.add)),
                       reads=[b_cA, b_cB], writes=[b_st], tag="ssa")
                sch.op("vector", (lambda e: e.tensor_tensor(
                    cC[:, :], cC[:, :], cD[:, :], OP.add)),
                       reads=[b_cC, b_cD], writes=[b_cC], tag="ssb")
                sch.op("vector", (lambda e: e.tensor_scalar(
                    st[:, :], st[:, :], hbar[:, 0:1], None, OP.mult)),
                       reads=[b_st, b_hbar], writes=[b_st], tag="ssc")
                sch.op("vector", (lambda e: e.scalar_tensor_tensor(
                    st[:, :], cC[:, :], env.hbc[:, 0:1], st[:, :],
                    OP.mult, OP.add)),
                       reads=[b_cC, env.b_hbc, b_st], writes=[b_st],
                       tag="ssd")
                sch.op("vector", (lambda e: e.tensor_copy(
                    stb[:, :], st[:, :])),
                       reads=[b_st], writes=[b_stb], tag="sse")
                sch.dma("sync", sv_out[:, j, :], stb[:, :], reads=[b_stb],
                        writes=[b_sdram], tag="ssf")

        yacc, b_yacc = sch.sbuf("yacc0", [128, NT, D], BF16)
        with sch.scope():
            mask_f, b_maskf = sch.sbuf("mk0f", [128, NT, 8], F32)
            mask_t, b_mask = sch.sbuf("mk0b", [128, NT, 8], BF16)
            gs, _ = sch.sbuf("gs0", [128, 4], F32)
            b_gs = sch.buf("gs0")
            mk1, b_mk1 = sch.sbuf("mk10", [128, 8], F32)
            sbv, b_sbv = sch.sbuf("sbv0", [128, 8], F32)
            for j in range(NT):
                sch.op("vector", (lambda e, jj=j: e.tensor_reduce(
                    gs[:, 0:1], logits[:, jj, :], AX.X, OP.max)),
                       reads=[b_logits], writes=[b_gs], tag="f0max")
                sch.op("vector", (lambda e, jj=j: e.tensor_scalar(
                    mk1[:, :], logits[:, jj, :], gs[:, 0:1], None,
                    OP.is_ge)),
                       reads=[b_logits, b_gs], writes=[b_mk1], tag="f0m1")
                sch.op("vector", (lambda e, jj=j: e.scalar_tensor_tensor(
                    sbv[:, :], mk1[:, :], -BIG, logits[:, jj, :],
                    OP.mult, OP.add)),
                       reads=[b_mk1, b_logits], writes=[b_sbv], tag="f0dr")
                sch.op("vector", (lambda e: e.tensor_reduce(
                    gs[:, 1:2], sbv[:, :], AX.X, OP.max)),
                       reads=[b_sbv], writes=[b_gs], tag="f0m2x")
                sch.op("vector", (lambda e: e.tensor_scalar(
                    sbv[:, :], sbv[:, :], gs[:, 1:2], None, OP.is_ge)),
                       reads=[b_sbv, b_gs], writes=[b_sbv], tag="f0m2")
                sch.op("vector", (lambda e, jj=j: e.tensor_tensor(
                    mask_f[:, jj, :], mk1[:, :], sbv[:, :], OP.add)),
                       reads=[b_mk1, b_sbv], writes=[b_maskf], tag="f0mk")
                sch.op("vector", (lambda e, jj=j: e.tensor_copy(
                    mask_t[:, jj, :], mask_f[:, jj, :])),
                       reads=[b_maskf], writes=[b_mask], tag="f0mc")
            mfmT, b_mfmT = sch.sbuf("mfmT0", [8, NT, 128], BF16)
            ps_mt, b_pmt = sch.psum("pmt0", [8, 128], BF16)
            for j in range(NT):
                sch.op("tensor", (lambda e, jj=j: e.transpose(
                    ps_mt[:, :], mask_t[:, jj, :], env.ident_bf[:, :])),
                       reads=[b_mask, env.b_ident_bf], writes=[b_pmt],
                       tag="f0tr")
                sch.op("vector", (lambda e, jj=j: e.tensor_copy(
                    mfmT[:, jj, :], ps_mt[:, :])),
                       reads=[b_pmt], writes=[b_mfmT], tag="f0te")
            ebf, b_ebf = sch.sbuf("eb0f", [8, D], F32)
            ebb, b_ebb = sch.sbuf("eb0b", [8, D], BF16)
            sch.dma("sync", ebf[:, :], W["L0_ffn_expert_b"], writes=[b_ebf],
                    tag="ebl")
            sch.op("vector", (lambda e: e.tensor_copy(ebb[:, :], ebf[:, :])),
                   reads=[b_ebf], writes=[b_ebb], tag="ebc")
            ps_f, b_psf = sch.psum("psf0", [128, 512], F32)
            for j in range(NT):
                for half in range(2):
                    sch.op("tensor", (lambda e, jj=j, h=half: e.matmul(
                        ps_f[:, :], mfmT[:, jj, :],
                        ebb[:, h * 512:(h + 1) * 512],
                        start=True, stop=True)),
                           reads=[b_mfmT, b_ebb], writes=[b_psf], tag="f0bm")
                    sch.op("vector", (lambda e, jj=j, h=half: e.tensor_copy(
                        yacc[:, jj, h * 512:(h + 1) * 512], ps_f[:, :])),
                           reads=[b_psf], writes=[b_yacc], tag="f0be")
            xnf, b_xnf = sch.sbuf("xnf0", [128, ND, T], BF16)
            sch.dma("sync", xnf[:, :, :],
                    d_xn.rearrange("p (c t2) -> p c t2", t2=T),
                    reads=[b_dxn], writes=[b_xnf], tag="xnfl0")
            wte, b_wte = sch.sbuf("wte0", [128, ND, D], BF16)
            for e_i in range(8):
                wed = W["L0_ffn_expert_w"][e_i:e_i + 1, :, :].rearrange(
                    "a b c -> (a b) c")
                w_transpose(env, wed, D, D, f"l0e{e_i}", dst=(wte, b_wte))
                for j in range(NT):
                    for half in range(2):
                        for cin in range(ND):
                            sch.op("tensor", (lambda e, jj=j, h=half,
                                              c_=cin: e.matmul(
                                ps_f[:, :], xnf[:, c_, jj * 128:(jj + 1) * 128],
                                wte[:, c_, h * 512:(h + 1) * 512],
                                start=(c_ == 0), stop=(c_ == ND - 1))),
                                   reads=[b_xnf, b_wte], writes=[b_psf],
                                   tag="f0em")
                        sch.op("vector", (lambda e, jj=j, h=half, ee=e_i:
                                          e.scalar_tensor_tensor(
                            yacc[:, jj, h * 512:(h + 1) * 512], ps_f[:, :],
                            mask_f[:, jj, ee:ee + 1],
                            yacc[:, jj, h * 512:(h + 1) * 512],
                            OP.mult, OP.add)),
                               reads=[b_psf, b_maskf, b_yacc],
                               writes=[b_yacc], tag="f0ea")

        emit_combine(env, xi, xi_buf, xo, xo_buf, s_dram, b_sdram,
                     yacc, b_yacc)


EMITTERS = {0: emit_layer0, 1: emit_layer1, 2: emit_layer2, 3: emit_layer3}



def build_program(layers):
    nc = bass.Bass("TRN2", target_bir_lowering=False)
    es = contextlib.ExitStack()
    sch = Sched(nc, es)
    env = Env()
    env.nc, env.sch = nc, sch

    def inp(name, shape, dt=F32):
        return nc.dram_tensor(name, list(shape), dt, kind="ExternalInput").ap()

    env.xin = inp("xin", [T, D])
    env.hflag = inp("hflag", [1, 1])
    hc = host_consts()
    env.consts_dram = {k: inp(k, list(v.shape),
                              BF16 if v.dtype == BFD else F32)
                       for k, v in hc.items()}
    need = set()
    for L in layers:
        need |= {n for (n, s) in INPUT_SPECS if n.startswith(f"L{L}_")}
    env.W = {n: inp(n, s) for (n, s) in INPUT_SPECS if n in need}
    env.need = need
    if 0 in layers:
        env.P = {n: inp(n, s) for (n, s) in L0_PARAM_SPECS}
    env.xout = nc.dram_tensor("xout", [T, D], F32, kind="ExternalOutput").ap()

    with es:
        env.xout_buf = sch.buf("xout")
        env.xin_buf = sch.buf("xin")
        env.ident_bf, env.b_ident_bf = sch.sbuf("identb", [128, 128], BF16)
        env.ident_f32, env.b_ident_f32 = sch.sbuf("identf", [128, 128], F32)
        env.hbc, env.b_hbc = sch.sbuf("hbc", [128, 1], F32)
        sch.dma("sync", env.ident_bf[:, :], env.consts_dram["ident_bf"],
                writes=[env.b_ident_bf])
        sch.dma("sync", env.ident_f32[:, :], env.consts_dram["ident_f32"],
                writes=[env.b_ident_f32])
        sch.dma("sync", env.hbc[:, :],
                env.hflag.rearrange("a b -> (a b)").partition_broadcast(128),
                writes=[env.b_hbc])

        x_chain = [(env.xin, env.xin_buf)]
        for i in range(len(layers) - 1):
            t = nc.dram_tensor(f"x_mid{i}", [T, D], F32).ap()
            x_chain.append((t, sch.buf(f"x_mid{i}")))
        x_chain.append((env.xout, env.xout_buf))

        for i, L in enumerate(layers):
            EMITTERS[L](env, x_chain[i][0], x_chain[i][1],
                        x_chain[i + 1][0], x_chain[i + 1][1])

        sch.op("gpsimd", lambda e: e.nop(),
               after=[env.xout_buf.last_writer], tag="final")
        sch.finalize()
    return nc


def make_in_maps(inputs, layers):
    hc = host_consts()
    x = np.ascontiguousarray(inputs["L0_attn_x"], dtype=np.float32)
    need = set()
    for L in layers:
        need |= {n for (n, s) in INPUT_SPECS if n.startswith(f"L{L}_")}
    wmap = {n: np.ascontiguousarray(inputs[n], dtype=np.float32) for n in need}
    in_maps = []
    l0p = [l0_core_params(inputs, h) for h in range(2)] if 0 in layers else None
    for c in range(8):
        b, h = c // 2, c % 2
        m = {"xin": np.ascontiguousarray(x[b, h * T:(h + 1) * T, :]),
             "hflag": np.array([[float(h)]], dtype=np.float32)}
        for k, v in hc.items():
            m[k] = v
        m.update(wmap)
        if l0p is not None:
            m.update(l0p[h])
        in_maps.append(m)
    return in_maps


def kernel(**inputs):
    layers = [0, 1, 2, 3]
    nc = build_program(layers)
    in_maps = make_in_maps(inputs, layers)
    res = run_bass_kernel_spmd(nc, in_maps, core_ids=list(range(8)))
    out = np.empty((B, S, D), dtype=np.float32)
    for c in range(8):
        b, h = c // 2, c % 2
        out[b, h * T:(h + 1) * T, :] = np.asarray(res.results[c]["xout"])
    return out
